# version 06
# speedup vs baseline: 1.1494x; 1.0174x over previous
import numpy as np
import ml_dtypes
import concourse.bass as bass
import concourse.mybir as mybir
from concourse.bass import IndirectOffsetOnAxis
from concourse.bass_utils import run_bass_kernel_spmd

F32 = mybir.dt.float32
BF16 = mybir.dt.bfloat16
AF = mybir.ActivationFunctionType
ALU = mybir.AluOpType
AX = mybir.AxisListType

NCORE = 8
SEQ = 2048
DM = 1024
NSEQ = 2
T = NSEQ * SEQ
NCH = DM // 128

S_E = 1536
S_TOT = 8 * S_E


class Sems:
    def __init__(self, nc):
        self.nc = nc
        self.h = {}
        self.v = {}

    def get(self, name):
        if name not in self.h:
            g = self.nc.semaphore(name)
            self.h[name] = g.__enter__()
            self.v[name] = 0
        return self.h[name]

    def base(self, name):
        self.get(name)
        return self.v[name]

    def inc(self, name, k):
        self.get(name)
        self.v[name] += k
        return self.h[name]


class P:
    PFX_CHANS = ("dS", "dP", "dP2")

    def __init__(self, nc):
        self.nc = nc
        self.sems = Sems(nc)
        self.sb = {}
        self.dram = {}
        self.host = {}
        self.host_percore = {}
        self.l1_guards = []
        self.pfx = ""

    def _nm(self, sem):
        return sem + self.pfx if sem in self.PFX_CHANS else sem

    def dma(self, q, out, in_, sem, **kw):
        h = self.sems.inc(self._nm(sem), 16)
        q.dma_start(out, in_, **kw).then_inc(h, 16)

    def cinc(self, inst, sem, k=1):
        sem = self._nm(sem)
        inst.then_inc(self.sems.inc(sem, k), k)

    def wge(self, eng, sem, val):
        eng.wait_ge(self.sems.get(self._nm(sem)), val)

    def wait_all(self, eng, sem):
        sem = self._nm(sem)
        eng.wait_ge(self.sems.get(sem), self.sems.v[sem])

    def base(self, sem):
        return self.sems.base(self._nm(sem))

    def val(self, sem):
        return self.sems.v[self._nm(sem)]


def _sbuf(nc, name, shape, dtype):
    return nc.sbuf_tensor(name, shape, dtype).__enter__()


def _sbuf_g(p, name, shape, dtype):
    g = p.nc.sbuf_tensor(name, shape, dtype)
    t = g.__enter__()
    p.l1_guards.append(g)
    return t


def _free_l1(p):
    for g in reversed(p.l1_guards):
        g.__exit__(None, None, None)
    p.l1_guards = []



def emit_entry(p):
    nc = p.nc
    x_in = p.dram["x_in"].ap()
    x_res = p.dram["x_res"].ap()
    ntile = T // 128
    with (
        nc.sbuf_tensor("en_in0", [128, DM], F32) as tin0,
        nc.sbuf_tensor("en_in1", [128, DM], F32) as tin1,
        nc.sbuf_tensor("en_st0", [128, DM], F32) as st0,
        nc.sbuf_tensor("en_st1", [128, DM], F32) as st1,
        nc.psum_tensor("en_p0", [128, 128], F32) as p0,
        nc.psum_tensor("en_p1", [128, 128], F32) as p1,
        nc.Block() as blk,
    ):
        ident = p.sb["ident_f32"]
        bL = p.base("dS")
        bM = p.base("mm")
        bE = p.base("ev")
        bO = p.base("dP")

        @blk.sync
        def _(sy):
            for it in range(ntile):
                s, i = divmod(it, SEQ // 128)
                tin = (tin0, tin1)[it % 2]
                if it >= 2:
                    p.wge(sy, "mm", bM + (it - 1) * NCH)
                p.dma(sy, tin[:, :], x_in[s, i * 128:(i + 1) * 128, :], "dS")

        @blk.tensor
        def _(t):
            for it in range(ntile):
                tin = (tin0, tin1)[it % 2]
                p.wge(t, "dS", bL + 16 * (it + 1))
                for c in range(NCH):
                    pp = (p0, p1)[c % 2]
                    k = it * NCH + c
                    if k >= 2:
                        p.wge(t, "ev", bE + k - 1)
                    mm = t.transpose(pp[:, :], tin[:, c * 128:(c + 1) * 128],
                                     ident[:, :])
                    p.cinc(mm, "mm", 1)

        @blk.vector
        def _(v):
            for it in range(ntile):
                st = (st0, st1)[it % 2]
                if it >= 2:
                    p.wge(v, "dP", bO + 16 * NCH * (it - 1))
                for c in range(NCH):
                    pp = (p0, p1)[c % 2]
                    k = it * NCH + c
                    p.wge(v, "mm", bM + k + 1)
                    cp = v.tensor_copy(st[:, c * 128:(c + 1) * 128], pp[:, :])
                    p.cinc(cp, "ev", 1)

        @blk.gpsimd
        def _(g):
            for it in range(ntile):
                st = (st0, st1)[it % 2]
                p.wge(g, "ev", bE + (it + 1) * NCH)
                for c in range(NCH):
                    p.dma(g, x_res[c * 128:(c + 1) * 128,
                                   it * 128:(it + 1) * 128],
                          st[:, c * 128:(c + 1) * 128], "dP")
            p.wait_all(g, "dP")


def emit_exit(p):
    nc = p.nc
    out = p.dram["out"].ap()
    x_res = p.dram["x_res"].ap()
    ntile = T // 128
    with (
        nc.sbuf_tensor("ex_in0", [128, DM], F32) as tin0,
        nc.sbuf_tensor("ex_in1", [128, DM], F32) as tin1,
        nc.sbuf_tensor("ex_st0", [128, DM], F32) as st0,
        nc.sbuf_tensor("ex_st1", [128, DM], F32) as st1,
        nc.psum_tensor("ex_p0", [128, 128], F32) as p0,
        nc.psum_tensor("ex_p1", [128, 128], F32) as p1,
        nc.Block() as blk,
    ):
        ident = p.sb["ident_f32"]
        bL = p.base("dS")
        bM = p.base("mm")
        bE = p.base("ev")
        bO = p.base("dP")

        @blk.sync
        def _(sy):
            for it in range(ntile):
                tin = (tin0, tin1)[it % 2]
                if it >= 2:
                    p.wge(sy, "mm", bM + (it - 1) * NCH)
                for c in range(NCH):
                    p.dma(sy, tin[:, c * 128:(c + 1) * 128],
                          x_res[c * 128:(c + 1) * 128,
                                it * 128:(it + 1) * 128], "dS")

        @blk.tensor
        def _(t):
            for it in range(ntile):
                tin = (tin0, tin1)[it % 2]
                p.wge(t, "dS", bL + 16 * NCH * (it + 1))
                for c in range(NCH):
                    pp = (p0, p1)[c % 2]
                    k = it * NCH + c
                    if k >= 2:
                        p.wge(t, "ev", bE + k - 1)
                    mm = t.transpose(pp[:, :], tin[:, c * 128:(c + 1) * 128],
                                     ident[:, :])
                    p.cinc(mm, "mm", 1)

        @blk.vector
        def _(v):
            for it in range(ntile):
                st = (st0, st1)[it % 2]
                if it >= 2:
                    p.wge(v, "dP", bO + 16 * (it - 1))
                for c in range(NCH):
                    pp = (p0, p1)[c % 2]
                    k = it * NCH + c
                    p.wge(v, "mm", bM + k + 1)
                    cp = v.tensor_copy(st[:, c * 128:(c + 1) * 128], pp[:, :])
                    p.cinc(cp, "ev", 1)

        @blk.gpsimd
        def _(g):
            for it in range(ntile):
                s, i = divmod(it, SEQ // 128)
                st = (st0, st1)[it % 2]
                p.wge(g, "ev", bE + (it + 1) * NCH)
                p.dma(g, out[s, i * 128:(i + 1) * 128, :], st[:, :], "dP")
            p.wait_all(g, "dP")



def emit_ln(p, li):
    nc = p.nc
    x_res = p.dram["x_res"].ap()
    xn_hbm = p.dram["xn_hbm"].ap()
    xn = [p.sb[f"xn{c}"] for c in range(NCH)]
    nsl = T // 512
    with (
        nc.sbuf_tensor(f"ln_x0_{li}", [128, T], F32) as xf0,
        nc.sbuf_tensor(f"ln_x1_{li}", [128, T], F32) as xf1,
        nc.sbuf_tensor(f"ln_sq0_{li}", [128, 512], BF16) as sq0,
        nc.sbuf_tensor(f"ln_sq1_{li}", [128, 512], BF16) as sq1,
        nc.sbuf_tensor(f"ln_sums_{li}", [1, T], F32) as sums,
        nc.sbuf_tensor(f"ln_sqs_{li}", [1, T], F32) as sqs,
        nc.sbuf_tensor(f"ln_m_{li}", [1, T], F32) as mrow,
        nc.sbuf_tensor(f"ln_r_{li}", [1, T], F32) as rrow,
        nc.sbuf_tensor(f"ln_mb_{li}", [128, T], BF16) as mb,
        nc.sbuf_tensor(f"ln_rb_{li}", [128, T], BF16) as rb,
        nc.psum_tensor(f"ln_pA_{li}", [128, 512], F32) as pA,
        nc.psum_tensor(f"ln_pB_{li}", [128, 512], F32) as pB,
    ):
        with nc.Block() as blk:
            bL = p.base("dS")
            bE = p.base("ev")

            @blk.sync
            def _(sy):
                for c in range(NCH):
                    xf = (xf0, xf1)[c % 2]
                    if c >= 2:
                        p.wge(sy, "ev", bE + c - 1)
                    p.dma(sy, xf[:, :], x_res[c * 128:(c + 1) * 128, :], "dS")

            @blk.vector
            def _(v):
                for c in range(NCH):
                    xf = (xf0, xf1)[c % 2]
                    p.wge(v, "dS", bL + 16 * (c + 1))
                    cp = v.tensor_copy(xn[c][:, :], xf[:, :])
                    p.cinc(cp, "ev", 1)

        with nc.Block() as blk:
            bQ = p.base("aq")
            bM = p.base("mm")
            bE = p.base("ev")
            bQC = p.base("mm2")

            @blk.scalar
            def _(s):
                for j in range(nsl):
                    for c in range(NCH):
                        sq = (sq0, sq1)[c % 2]
                        k = j * NCH + c
                        if k >= 2:
                            p.wge(s, "mm2", bQC + k - 1)
                        a = s.activation(sq[:, :],
                                         xn[c][:, j * 512:(j + 1) * 512],
                                         AF.Square)
                        p.cinc(a, "aq", 1)

            @blk.tensor
            def _(t):
                ones_col = p.sb["ones_col_bf16"]
                for j in range(nsl):
                    if j >= 1:
                        p.wge(t, "ev", bE + j)
                    for c in range(NCH):
                        t.matmul(pA[0:1, :], ones_col[:, :],
                                 xn[c][:, j * 512:(j + 1) * 512],
                                 start=(c == 0), stop=(c == NCH - 1))
                    for c in range(NCH):
                        sq = (sq0, sq1)[c % 2]
                        k = j * NCH + c
                        p.wge(t, "aq", bQ + k + 1)
                        mm2 = t.matmul(pB[0:1, :], ones_col[:, :], sq[:, :],
                                       start=(c == 0), stop=(c == NCH - 1))
                        p.cinc(mm2, "mm2", 1)

            @blk.vector
            def _(v):
                for j in range(nsl):
                    p.wge(v, "mm2", bQC + (j + 1) * NCH)
                    v.tensor_copy(sums[0:1, j * 512:(j + 1) * 512], pA[0:1, :])
                    cp = v.tensor_copy(sqs[0:1, j * 512:(j + 1) * 512],
                                       pB[0:1, :])
                    p.cinc(cp, "ev", 1)

        with nc.Block() as blk:
            bV = p.base("ev")
            bA = p.base("aq")

            @blk.vector
            def _(v):
                v.tensor_scalar_mul(mrow[0:1, :], sums[0:1, :], 1.0 / DM)
                st = v.scalar_tensor_tensor(rrow[0:1, :], sums[0:1, :],
                                            1.0 / (DM * DM), sums[0:1, :],
                                            ALU.mult, ALU.mult)
                p.cinc(st, "x1", 1)
                p.wge(v, "x1", p.val("x1"))
                st = v.scalar_tensor_tensor(rrow[0:1, :], sqs[0:1, :],
                                            1.0 / DM, rrow[0:1, :],
                                            ALU.mult, ALU.subtract)
                p.cinc(st, "ev", 1)
                p.wge(v, "aq", bA + 1)
                cp = v.reciprocal(rrow[0:1, :], rrow[0:1, :])
                p.cinc(cp, "ev", 1)

            @blk.scalar
            def _(s):
                p.wge(s, "ev", bV + 1)
                eps = p.sb["c_eps_ln"]
                a = s.activation(rrow[0:1, :], rrow[0:1, :], AF.Sqrt,
                                 bias=eps[0:1, 0:1])
                p.cinc(a, "aq", 1)

        with nc.Block() as blk:
            bM = p.base("mm")
            bE = p.base("ev")

            @blk.tensor
            def _(t):
                ones_row = p.sb["ones_row_f32"]
                for j in range(nsl):
                    if j >= 1:
                        p.wge(t, "ev", bE + j)
                    t.matmul(pA[:, :], ones_row[:, :],
                             mrow[0:1, j * 512:(j + 1) * 512],
                             start=True, stop=True)
                    mm2 = t.matmul(pB[:, :], ones_row[:, :],
                                   rrow[0:1, j * 512:(j + 1) * 512],
                                   start=True, stop=True)
                    p.cinc(mm2, "mm", 1)

            @blk.vector
            def _(v):
                for j in range(nsl):
                    p.wge(v, "mm", bM + j + 1)
                    v.tensor_copy(mb[:, j * 512:(j + 1) * 512], pA[:, :])
                    cp = v.tensor_copy(rb[:, j * 512:(j + 1) * 512], pB[:, :])
                    p.cinc(cp, "ev", 1)

        with nc.Block() as blk:
            bE = p.base("ev")

            @blk.vector
            def _(v):
                for c in range(NCH):
                    v.tensor_sub(xn[c][:, :], xn[c][:, :], mb[:, :])
                    cp = v.tensor_mul(xn[c][:, :], xn[c][:, :], rb[:, :])
                    p.cinc(cp, "ev", 1)

            @blk.gpsimd
            def _(g):
                for c in range(NCH):
                    p.wge(g, "ev", bE + c + 1)
                    p.dma(g, xn_hbm[c * 128:(c + 1) * 128, :], xn[c][:, :],
                          "dP")
                p.wait_all(g, "dP")



def _host_consts():
    h = {}
    h["ident_f32"] = np.eye(128, dtype=np.float32)
    h["ident_bf16"] = np.eye(128, dtype=ml_dtypes.bfloat16)
    h["triu_bf16"] = np.triu(np.ones((128, 128))).astype(ml_dtypes.bfloat16)
    half = 128
    inv_freq = (1.0 / (10000.0 ** (np.arange(half, dtype=np.float32) / half))
                ).astype(np.float32)
    ang = (np.arange(SEQ, dtype=np.float32)[None, :]
           * inv_freq[:, None]).astype(np.float32)
    h["rope_cos"] = np.cos(ang).astype(np.float32)
    h["rope_sin"] = np.sin(ang).astype(np.float32)
    e4 = np.zeros((4, 512), dtype=ml_dtypes.bfloat16)
    for r in range(4):
        e4[r, r * 128:(r + 1) * 128] = 1
    h["eye4r"] = e4
    e8 = np.zeros((8, 1024), dtype=ml_dtypes.bfloat16)
    for r in range(8):
        e8[r, r * 128:(r + 1) * 128] = 1
    h["eye8e"] = e8
    return h


CONST_SHAPES = {
    "ident_f32": ([128, 128], F32),
    "ident_bf16": ([128, 128], BF16),
    "triu_bf16": ([128, 128], BF16),
    "rope_cos": ([128, SEQ], F32),
    "rope_sin": ([128, SEQ], F32),
    "eye4r": ([4, 512], BF16),
    "eye8e": ([8, 1024], BF16),
}


def declare_io(p):
    nc = p.nc
    d = p.dram
    d["x_in"] = nc.dram_tensor("x_in", [NSEQ, SEQ, DM], F32,
                               kind="ExternalInput")
    d["out"] = nc.dram_tensor("out", [NSEQ, SEQ, DM], F32,
                              kind="ExternalOutput")
    d["x_res"] = nc.dram_tensor("x_res", [DM, T], F32, kind="Internal")
    d["xn_hbm"] = nc.dram_tensor("xn_hbm", [DM, T], BF16, kind="Internal")
    for nm, (shp, dt) in CONST_SHAPES.items():
        d[nm] = nc.dram_tensor(nm, shp, dt, kind="ExternalInput")


def emit_setup(p):
    nc = p.nc
    sb = p.sb
    sb["ident_f32"] = _sbuf(nc, "sb_ident_f32", [128, 128], F32)
    sb["ident_bf16"] = _sbuf(nc, "sb_ident_bf16", [128, 128], BF16)
    sb["triu_bf16"] = _sbuf(nc, "sb_triu", [128, 128], BF16)
    sb["ones_col_bf16"] = _sbuf(nc, "sb_ones_col", [128, 1], BF16)
    sb["ones_col_f32"] = _sbuf(nc, "sb_ones_col_f32", [128, 1], F32)
    sb["ones_row_f32"] = _sbuf(nc, "sb_ones_row_f32", [1, 128], F32)
    sb["ones_row_bf16"] = _sbuf(nc, "sb_ones_row_bf16", [1, 128], BF16)
    sb["c_eps_ln"] = _sbuf(nc, "sb_eps_ln", [1, 1], F32)
    sb["eye4r"] = _sbuf(nc, "sb_eye4r", [4, 512], BF16)
    sb["eye8e"] = _sbuf(nc, "sb_eye8e", [8, 1024], BF16)
    for c in range(NCH):
        sb[f"xn{c}"] = _sbuf(nc, f"sb_xn{c}", [128, T], BF16)
    p.ps = [nc.psum_tensor("ps_g0", [128, 512], F32).__enter__(),
            nc.psum_tensor("ps_g1", [128, 512], F32).__enter__()]
    with nc.Block() as blk:
        @blk.gpsimd
        def _(g):
            for nm in ["ident_f32", "ident_bf16", "triu_bf16", "eye4r",
                       "eye8e"]:
                p.dma(g, sb[nm][:, :], p.dram[nm].ap(), "dP")
            g.memset(sb["ones_col_bf16"][:, :], 1.0)
            g.memset(sb["ones_col_f32"][:, :], 1.0)
            g.memset(sb["ones_row_f32"][:, :], 1.0)
            g.memset(sb["ones_row_bf16"][:, :], 1.0)
            g.memset(sb["c_eps_ln"][:, :], 1e-5)
            p.wait_all(g, "dP")



def _noop_emit(p):
    pass


_PV_N = [0]


def emit_pervec(p, src_name, n, dst, src_ap=None):
    nc = p.nc
    m = n // 128
    _PV_N[0] += 1
    with (
        nc.sbuf_tensor(f"pv_{src_name}{_PV_N[0]}", [m, 128], F32) as stg,
        nc.psum_tensor(f"pvp_{src_name}{_PV_N[0]}", [128, m], F32) as ps,
        nc.Block() as blk,
    ):
        bL = p.base("dS")
        bM = p.base("mm")

        @blk.sync
        def _(sy):
            src = p.dram[src_name].ap() if src_ap is None else src_ap
            p.dma(sy, stg[:, :], src.rearrange("(m k) -> m k", k=128), "dS")

        @blk.tensor
        def _(t):
            p.wge(t, "dS", bL + 16)
            mm = t.transpose(ps[:, :], stg[:, :],
                             p.sb["ident_f32"][0:m, 0:m])
            p.cinc(mm, "mm", 1)

        @blk.vector
        def _(v):
            p.wge(v, "mm", bM + 1)
            v.tensor_copy(dst[:, :], ps[:, :])


def emit_pervec2(p, src_name, rows, cols, dst):
    nc = p.nc
    with (
        nc.sbuf_tensor(f"pv2_{src_name}", [rows, cols], F32) as stg,
        nc.psum_tensor(f"pv2p_{src_name}", [cols, rows], F32) as ps,
        nc.Block() as blk,
    ):
        bL = p.base("dS")
        bM = p.base("mm")

        @blk.sync
        def _(sy):
            src = p.dram[src_name].ap()
            p.dma(sy, stg[:, :], src.rearrange("(m k) -> m k", k=cols), "dS")

        @blk.tensor
        def _(t):
            p.wge(t, "dS", bL + 16)
            mm = t.transpose(ps[:, :], stg[:, :],
                             p.sb["ident_f32"][0:rows, 0:rows])
            p.cinc(mm, "mm", 1)

        @blk.vector
        def _(v):
            p.wge(v, "mm", bM + 1)
            v.tensor_copy(dst[:, :], ps[:, :])


def attn_units():
    out = []
    for qb in range(4):
        out.append((qb, list(range(4 * qb + 4)), [4 * qb + j for j in range(4)]))
    return out


def emit_attn_core(p, tag, seqs, nh, get_k, get_q, get_v, o_slc, o_store,
                   sync_build, n_build_dma, pt, rrow, rb):
    nc = p.nc
    units = attn_units()
    VIS = [len(u[1]) for u in units]
    CVIS = [0]
    for v_ in VIS:
        CVIS.append(CVIS[-1] + v_)
    UPS = CVIS[-1]
    nsh = len(seqs) * nh

    def sh2 (sh):
        return seqs[sh // nh], sh % nh

    with (
        nc.psum_tensor(f"{tag}_psA", [128, 512], F32) as psA,
        nc.psum_tensor(f"{tag}_psB", [128, 512], F32) as psB,
        nc.psum_tensor(f"{tag}_psR", [1, 512], F32) as psR,
        nc.psum_tensor(f"{tag}_psC", [64, 512], F32) as psC,
        nc.psum_tensor(f"{tag}_psO", [64, 512], F32) as psO,
        nc.Block() as blk,
    ):
        bS = p.base("dS")
        bMM = p.base("mm")
        bAQ = p.base("aq")
        bPL = p.base("pl")
        bM2 = p.base("mm2")
        bEV = p.base("ev")
        bM3 = p.base("mm3")
        bE2 = p.base("ev2")
        bM4 = p.base("mm4")
        bE3 = p.base("ev3")
        bP = p.base("dP")

        @blk.sync
        def _(sy):
            if n_build_dma == 0:
                return
            for sh in range(nsh):
                s, h = sh2(sh)
                if sh >= 2:
                    p.wge(sy, "mm", bMM + UPS * (sh - 1))
                sync_build(sy, s, h)

        @blk.tensor
        def _(t):
            ones_col = p.sb["ones_col_bf16"]
            ones_row = p.sb["ones_row_f32"]
            for sh in range(nsh):
                s, h = sh2(sh)
                if n_build_dma:
                    p.wge(t, "dS", bS + 16 * n_build_dma * (sh + 1))
                for qb, kts, partial in units:
                    u = sh * 4 + qb
                    for i, kt in enumerate(kts):
                        ps = (psA, psB)[i % 2]
                        w = bAQ + UPS * sh + CVIS[qb] + i - 1
                        if w > bAQ:
                            p.wge(t, "aq", w)
                        mm = t.matmul(ps[:, :], get_k(s, h, kt),
                                      get_q(s, h, qb), start=True, stop=True)
                        p.cinc(mm, "mm", 1)
                    p.wge(t, "aq", bAQ + UPS * sh + CVIS[qb + 1])
                    p.wge(t, "pl", bPL + 4 * (u + 1))
                    if u >= 1:
                        p.wge(t, "ev", bEV + u)
                    for i, kt in enumerate(kts):
                        mm = t.matmul(psR[0:1, :], ones_col[:, :],
                                      pt[:, kt * 512:(kt + 1) * 512],
                                      start=(i == 0), stop=(i == len(kts) - 1))
                    p.cinc(mm, "mm2", 1)
                    p.wge(t, "ev", bEV + u + 1)
                    if u >= 1:
                        p.wge(t, "ev2", bE2 + u)
                    mm = t.matmul(psC[:, :], ones_row[0:1, 0:64],
                                  rrow[0:1, :], start=True, stop=True)
                    p.cinc(mm, "mm3", 1)
                    if u >= 1:
                        p.wge(t, "ev3", bE3 + u)
                    for i, kt in enumerate(kts):
                        mm = t.matmul(psO[:, :], get_v(s, h, kt),
                                      pt[:, kt * 512:(kt + 1) * 512],
                                      start=(i == 0), stop=(i == len(kts) - 1))
                    p.cinc(mm, "mm4", 1)

        @blk.scalar
        def _(sc):
            for sh in range(nsh):
                for qb, kts, partial in units:
                    u = sh * 4 + qb
                    for i, kt in enumerate(kts):
                        ps = (psA, psB)[i % 2]
                        if i == 0 and u >= 1:
                            p.wge(sc, "mm4", bM4 + u)
                        p.wge(sc, "mm", bMM + UPS * sh + CVIS[qb] + i + 1)
                        a = sc.activation(pt[:, kt * 512:(kt + 1) * 512],
                                          ps[:, :], AF.Exp)
                        p.cinc(a, "aq", 1)

        @blk.vector
        def _(v):
            for sh in range(nsh):
                for qb, kts, partial in units:
                    u = sh * 4 + qb
                    p.wge(v, "mm2", bM2 + u + 1)
                    cp = v.reciprocal(rrow[0:1, :], psR[0:1, :])
                    p.cinc(cp, "ev", 1)
                    p.wge(v, "mm3", bM3 + u + 1)
                    cp = v.tensor_copy(rb[:, :], psC[:, :])
                    p.cinc(cp, "ev2", 1)
                    if qb == 0 and sh >= 2:
                        p.wge(v, "dP", bP + 16 * (sh - 1))
                    p.wge(v, "mm4", bM4 + u + 1)
                    p.wge(v, "ev2", bE2 + u + 1)
                    cp = v.tensor_mul(
                        o_slc(sh % 2, qb * 512, (qb + 1) * 512),
                        psO[:, :], rb[:, :])
                    p.cinc(cp, "ev3", 1)

        @blk.gpsimd
        def _(g):
            for sh in range(nsh):
                s, h = sh2(sh)
                for qb, kts, partial in units:
                    u = sh * 4 + qb
                    for kt in partial:
                        p.wge(g, "aq", bAQ + UPS * sh + CVIS[qb] + kt + 1)
                        a = g.affine_select(
                            pt[:, kt * 512:(kt + 1) * 512],
                            pt[:, kt * 512:(kt + 1) * 512],
                            [[1, 512]], ALU.is_ge, 0.0,
                            base=qb * 512 - kt * 128, channel_multiplier=-1)
                        p.cinc(a, "pl", 1)
                p.wge(g, "ev3", bE3 + 4 * (sh + 1))
                o_store(g, s, h, o_slc(sh % 2, 0, SEQ))
            p.wait_all(g, "dP")



def emit_l0_attn(p):
    nc = p.nc
    xn = [p.sb[f"xn{c}"] for c in range(NCH)]
    o_scr = p.dram["o_scr"].ap()
    x_res = p.dram["x_res"].ap()
    H = 16
    from contextlib import ExitStack
    with ExitStack() as stck:
        en = lambda *a: stck.enter_context(nc.sbuf_tensor(*a))
        wqk_sb = en("l0_wqk", [128, NCH * 96], BF16)
        wv_sb = en("l0_wv", [128, NCH * 1024], BF16)
        Dcm = en("l0_D", [96, 32], BF16)
        sc1 = en("l0_scc", [1, 1], F32)
        scb = en("l0_scb", [128, 1], F32)
        qk3 = en("l0_qk3", [96, T], BF16)
        qkd = en("l0_qkd", [32, T], BF16)
        v_sb = en("l0_v", [128, 16 * 1024], BF16)
        pt = en("l0_pt", [128, 16 * 512], BF16)
        rrow = en("l0_rr", [1, 512], F32)
        rb = en("l0_rb", [64, 512], F32)
        ohb = en("l0_oh", [128, SEQ], BF16)
        qa0 = en("l0_qa0", [4, SEQ], BF16)
        qa1 = en("l0_qa1", [4, SEQ], BF16)
        ka0 = en("l0_ka0", [4, SEQ], BF16)
        ka1 = en("l0_ka1", [4, SEQ], BF16)

        with nc.Block() as blk:
            bP_ = p.base("dP")
            bMM = p.base("mm")
            nload = NCH * 3 + 32 + 1

            @blk.gpsimd
            def _(g):
                wq = p.dram["l0_wq_cm"].ap()
                wk = p.dram["l0_wk_cm"].ap()
                wv = p.dram["L0_attn_wv"].ap()
                hd = p.dram["L0_attn_head_dirs"].ap()
                sc = p.dram["L0_attn_scale"].ap()
                for c in range(NCH):
                    p.dma(g, wqk_sb[:, c * 96:c * 96 + 48],
                          wq[c * 128:(c + 1) * 128, :], "dP")
                    p.dma(g, wqk_sb[:, c * 96 + 48:c * 96 + 96],
                          wk[c * 128:(c + 1) * 128, :], "dP")
                    p.dma(g, wv_sb[:, c * 1024:(c + 1) * 1024],
                          wv[c * 128:(c + 1) * 128, :], "dP")
                g.memset(Dcm[:, :], 0.0)
                for h in range(16):
                    p.dma(g, Dcm[h:h + 33:16, h:h + 1], hd[h:h + 1, 0:3], "dP")
                    p.dma(g, Dcm[48 + h:48 + h + 33:16, 16 + h:17 + h],
                          hd[h:h + 1, 0:3], "dP")
                p.dma(g, sc1[:, :], sc.rearrange("(o a) -> o a", o=1), "dP")

            @blk.tensor
            def _(t):
                p.wge(t, "dP", bP_ + 16 * nload)
                mm = t.matmul(p.ps[0][0:128, 0:1], p.sb["ones_row_f32"][:, :],
                              sc1[0:1, 0:1], start=True, stop=True)
                p.cinc(mm, "mm", 1)

            @blk.vector
            def _(v):
                p.wge(v, "mm", bMM + 1)
                v.tensor_copy(scb[:, :], p.ps[0][0:128, 0:1])

        with nc.Block() as blk:
            bMM = p.base("mm")
            bEV = p.base("ev")
            nsl = T // 512
            ps0, ps1 = p.ps[0], p.ps[1]

            @blk.tensor
            def _(t):
                for j in range(nsl):
                    ps = (ps0, ps1)[j % 2]
                    if j >= 2:
                        p.wge(t, "ev", bEV + j - 1)
                    for c in range(NCH):
                        mm = t.matmul(ps[0:96, :],
                                      wqk_sb[:, c * 96:(c + 1) * 96],
                                      xn[c][:, j * 512:(j + 1) * 512],
                                      start=(c == 0), stop=(c == NCH - 1))
                    p.cinc(mm, "mm", 1)
                for j in range(nsl):
                    k = nsl + j
                    ps = (ps0, ps1)[k % 2]
                    p.wge(t, "ev", bEV + k - 1)
                    mm = t.matmul(ps[0:32, :], Dcm[:, :],
                                  qk3[:, j * 512:(j + 1) * 512],
                                  start=True, stop=True)
                    p.cinc(mm, "mm", 1)

            @blk.vector
            def _(v):
                for j in range(nsl):
                    ps = (ps0, ps1)[j % 2]
                    p.wge(v, "mm", bMM + j + 1)
                    cp = v.tensor_copy(qk3[:, j * 512:(j + 1) * 512],
                                       ps[0:96, :])
                    p.cinc(cp, "ev", 1)
                for j in range(nsl):
                    k = nsl + j
                    ps = (ps0, ps1)[k % 2]
                    p.wge(v, "mm", bMM + k + 1)
                    cp = v.tensor_copy(qkd[:, j * 512:(j + 1) * 512],
                                       ps[0:32, :])
                    p.cinc(cp, "ev", 1)
                p.wge(v, "ev", p.val("ev"))
                v.tensor_scalar_mul(qk3[0:48, :], qk3[0:48, :], scb[0:48, 0:1])

        for s in range(NSEQ):
            with nc.Block() as blk:
                bMM = p.base("mm")
                bEV = p.base("ev")
                ps0, ps1 = p.ps[0], p.ps[1]

                @blk.tensor
                def _(t):
                    k = 0
                    for tt in range(SEQ // 128):
                        for hf in range(2):
                            ps = (ps0, ps1)[k % 2]
                            if k >= 2:
                                p.wge(t, "ev", bEV + k - 1)
                            for c in range(NCH):
                                mm = t.matmul(
                                    ps[:, :],
                                    xn[c][:, s * SEQ + tt * 128:
                                          s * SEQ + (tt + 1) * 128],
                                    wv_sb[:, c * 1024 + hf * 512:
                                          c * 1024 + (hf + 1) * 512],
                                    start=(c == 0), stop=(c == NCH - 1))
                            p.cinc(mm, "mm", 1)
                            k += 1

                @blk.vector
                def _(v):
                    k = 0
                    for tt in range(SEQ // 128):
                        for hf in range(2):
                            ps = (ps0, ps1)[k % 2]
                            p.wge(v, "mm", bMM + k + 1)
                            cp = v.tensor_copy(
                                v_sb[:, tt * 1024 + hf * 512:
                                     tt * 1024 + (hf + 1) * 512], ps[:, :])
                            p.cinc(cp, "ev", 1)
                            k += 1

            def get_k(s_, h, kt):
                ka = (ka0, ka1)[h % 2]
                return ka[0:4, kt * 128:(kt + 1) * 128]

            def get_q(s_, h, qb):
                qa = (qa0, qa1)[h % 2]
                return qa[0:4, qb * 512:(qb + 1) * 512]

            def get_v(s_, h, kt):
                return v_sb[:, kt * 1024 + h * 64:kt * 1024 + h * 64 + 64]

            def sync_build(sy, s_, h):
                qa = (qa0, qa1)[h % 2]
                ka = (ka0, ka1)[h % 2]
                t0, t1 = s_ * SEQ, (s_ + 1) * SEQ
                for c in range(3):
                    p.dma(sy, qa[c:c + 1, :],
                          qk3[16 * c + h:16 * c + h + 1, t0:t1], "dS")
                    p.dma(sy, ka[c:c + 1, :],
                          qk3[48 + 16 * c + h:48 + 16 * c + h + 1, t0:t1],
                          "dS")
                p.dma(sy, qa[3:4, :], qkd[h:h + 1, t0:t1], "dS")
                p.dma(sy, ka[3:4, :], qkd[16 + h:17 + h, t0:t1], "dS")

            def o_slc(par, lo, hi):
                return ohb[64 * par:64 * par + 64, lo:hi]

            def o_store(g, s_, h, buf):
                p.dma(g, o_scr[64 * h:64 * h + 64, s_ * SEQ:(s_ + 1) * SEQ],
                      buf, "dP")

            emit_attn_core(p, f"l0a{s}", [s], H, get_k, get_q, get_v, o_slc,
                           o_store, sync_build, 8, pt, rrow, rb)

    emit_oproj(p, "l0o", "L0_attn_wo", "o_scr", bias_col=None)


def emit_oproj(p, tag, wo_name, o_name, bias_col):
    nc = p.nc
    x_res = p.dram["x_res"].ap()
    o_scr = p.dram[o_name].ap()
    nts = T // 512
    with (
        nc.sbuf_tensor(f"{tag}_wo", [128, NCH * 1024], BF16) as wo_sb,
        nc.sbuf_tensor(f"{tag}_oa0", [128, NCH * 512], BF16) as oa0,
        nc.sbuf_tensor(f"{tag}_oa1", [128, NCH * 512], BF16) as oa1,
        nc.sbuf_tensor(f"{tag}_or0", [128, NCH * 512], F32) as or0,
        nc.sbuf_tensor(f"{tag}_or1", [128, NCH * 512], F32) as or1,
        nc.Block() as blk,
    ):
        bS = p.base("dS")
        bL = p.base("dP")
        bM = p.base("mm")
        bE = p.base("ev")
        bA = p.base("dP2")

        @blk.gpsimd
        def _(g):
            wo = p.dram[wo_name].ap()
            for c in range(NCH):
                p.dma(g, wo_sb[:, c * 1024:(c + 1) * 1024],
                      wo[c * 128:(c + 1) * 128, :], "dP")

            def load(ts):
                oa = (oa0, oa1)[ts % 2]
                for c in range(NCH):
                    p.dma(g, oa[:, c * 512:(c + 1) * 512],
                          o_scr[c * 128:(c + 1) * 128,
                                ts * 512:(ts + 1) * 512], "dP")
            load(0)
            for ts in range(nts):
                if ts + 1 < nts:
                    if ts + 1 >= 2:
                        p.wge(g, "mm", bM + NCH * ts)
                    load(ts + 1)
                p.wge(g, "ev", bE + NCH * (ts + 1))
                orow = (or0, or1)[ts % 2]
                for j in range(NCH):
                    p.dma(g, x_res[j * 128:(j + 1) * 128,
                                   ts * 512:(ts + 1) * 512],
                          orow[:, j * 512:(j + 1) * 512], "dP2",
                          accum_op=ALU.add)
            p.wait_all(g, "dP2")

        @blk.tensor
        def _(t):
            p.wge(t, "dP", bL + 16 * NCH)
            for ts in range(nts):
                oa = (oa0, oa1)[ts % 2]
                p.wge(t, "dP", bL + 16 * NCH + 16 * NCH * (ts + 1))
                for j in range(NCH):
                    ps = p.ps[j % 2]
                    k = ts * NCH + j
                    if k >= 2:
                        p.wge(t, "ev", bE + k - 1)
                    for c in range(NCH):
                        mm = t.matmul(ps[:, :],
                                      wo_sb[:, c * 1024 + j * 128:
                                            c * 1024 + (j + 1) * 128],
                                      oa[:, c * 512:(c + 1) * 512],
                                      start=(c == 0), stop=(c == NCH - 1))
                    p.cinc(mm, "mm", 1)

        @blk.vector
        def _(v):
            for ts in range(nts):
                orow = (or0, or1)[ts % 2]
                for j in range(NCH):
                    ps = p.ps[j % 2]
                    p.wge(v, "mm", bM + ts * NCH + j + 1)
                    if ts >= 2 and j == 0:
                        p.wge(v, "dP2", bA + 16 * NCH * (ts - 1))
                    if bias_col is None:
                        cp = v.tensor_copy(orow[:, j * 512:(j + 1) * 512],
                                           ps[:, :])
                    else:
                        cp = v.tensor_scalar_add(
                            orow[:, j * 512:(j + 1) * 512], ps[:, :],
                            bias_col[:, j:j + 1])
                    p.cinc(cp, "ev", 1)



def emit_quantize_ternary(p, tag, wt_name, nrows, ncols, dst_name, eps):
    nc = p.nc
    nch = nrows // 128
    src = p.dram[wt_name].ap()
    dst = p.dram[dst_name].ap()
    with (
        nc.sbuf_tensor(f"{tag}_w0", [128, ncols], F32) as w0,
        nc.sbuf_tensor(f"{tag}_w1", [128, ncols], F32) as w1,
        nc.sbuf_tensor(f"{tag}_g", [128, nch], F32) as gcol,
        nc.sbuf_tensor(f"{tag}_g1", [128, 1], F32) as g128,
        nc.sbuf_tensor(f"{tag}_tau", [1, 1], F32) as tau,
        nc.sbuf_tensor(f"{tag}_tc", [128, 2], F32) as tcol,
        nc.sbuf_tensor(f"{tag}_qa", [128, ncols], BF16) as qa,
        nc.sbuf_tensor(f"{tag}_qb", [128, ncols], BF16) as qb,
        nc.sbuf_tensor(f"{tag}_qc0", [128, ncols], BF16) as qc0,
        nc.sbuf_tensor(f"{tag}_qc1", [128, ncols], BF16) as qc1,
    ):
        with nc.Block() as blk:
            bS = p.base("dS")
            bE = p.base("ev")
            bM = p.base("mm")

            @blk.sync
            def _(sy):
                for c in range(nch):
                    w = (w0, w1)[c % 2]
                    if c >= 2:
                        p.wge(sy, "ev", bE + c - 1)
                    p.dma(sy, w[:, :], src[c * 128:(c + 1) * 128, :], "dS")

            @blk.vector
            def _(v):
                for c in range(nch):
                    w = (w0, w1)[c % 2]
                    p.wge(v, "dS", bS + 16 * (c + 1))
                    cp = v.reduce_sum(gcol[:, c:c + 1], w[:, :], AX.X,
                                      apply_absolute_value=True)
                    p.cinc(cp, "ev", 1)
                p.wge(v, "ev", bE + nch)
                cp = v.reduce_sum(g128[:, 0:1], gcol[:, :], AX.X)
                p.cinc(cp, "ev", 1)

            @blk.tensor
            def _(t):
                ones_col = p.sb["ones_col_f32"]
                p.wge(t, "ev", bE + nch + 1)
                mm = t.matmul(p.ps[0][0:1, 0:1], g128[:, 0:1],
                              ones_col[:, :], start=True, stop=True)
                p.cinc(mm, "mm", 1)

        with nc.Block() as blk:
            bM = p.base("mm")
            bE = p.base("ev")

            @blk.vector
            def _(v):
                cp = v.tensor_scalar(tau[0:1, 0:1], p.ps[0][0:1, 0:1],
                                     0.5 / (nrows * ncols), 0.5 * eps,
                                     ALU.mult, ALU.add)
                p.cinc(cp, "ev", 1)
                p.wge(v, "mm", bM + 1)
                v.tensor_copy(tcol[:, 0:1], p.ps[0][0:128, 0:1])
                v.tensor_scalar_mul(tcol[:, 1:2], p.ps[0][0:128, 0:1], -1.0)

            @blk.tensor
            def _(t):
                p.wge(t, "ev", bE + 1)
                mm = t.matmul(p.ps[0][0:128, 0:1], p.sb["ones_row_f32"][:, :],
                              tau[0:1, 0:1], start=True, stop=True)
                p.cinc(mm, "mm", 1)

        with nc.Block() as blk:
            bS = p.base("dS")
            bE = p.base("ev")
            bP_ = p.base("dP")

            @blk.sync
            def _(sy):
                for c in range(nch):
                    w = (w0, w1)[c % 2]
                    if c >= 2:
                        p.wge(sy, "ev", bE + 3 * (c - 1))
                    p.dma(sy, w[:, :], src[c * 128:(c + 1) * 128, :], "dS")

            @blk.vector
            def _(v):
                for c in range(nch):
                    w = (w0, w1)[c % 2]
                    qc = (qc0, qc1)[c % 2]
                    p.wge(v, "dS", bS + 16 * (c + 1))
                    if c >= 2:
                        p.wge(v, "dP", bP_ + 16 * (c - 1))
                    cp = v.tensor_single_scalar(qa[:, :], w[:, :],
                                                tcol[:, 0:1], ALU.is_gt)
                    p.cinc(cp, "x1", 1)
                    cp = v.tensor_single_scalar(qb[:, :], w[:, :],
                                                tcol[:, 1:2], ALU.is_lt)
                    p.cinc(cp, "x1", 1)
                    p.wge(v, "x1", p.val("x1"))
                    cp = v.tensor_sub(qc[:, :], qa[:, :], qb[:, :])
                    p.cinc(cp, "ev", 3)

            @blk.gpsimd
            def _(g):
                for c in range(nch):
                    qc = (qc0, qc1)[c % 2]
                    p.wge(g, "ev", bE + 3 * (c + 1))
                    p.dma(g, dst[c * 128:(c + 1) * 128, :], qc[:, :], "dP")
                p.wait_all(g, "dP")


def emit_ffn2(p, tag, w1_name, w2_name, DH, b1c, b2c, act, xsrc=None,
              scale1=None):
    nc = p.nc
    x_res = p.dram["x_res"].ap()
    w1 = p.dram[w1_name].ap()
    w2 = p.dram[w2_name].ap()
    xs = xsrc if xsrc is not None else [p.sb[f"xn{c}"] for c in range(NCH)]
    HH = DH // 2
    nhh = HH // 128
    TB = 512
    ntb = T // TB
    gph = nhh + NCH
    afun = {"relu": AF.Relu, "gelu": AF.Gelu,
            "gelu_tanh": AF.Gelu_apprx_tanh}[act]
    with (
        nc.sbuf_tensor(f"{tag}_wb", [128, NCH * HH], BF16) as wb,
        nc.sbuf_tensor(f"{tag}_w2b", [128, nhh * DM], BF16) as w2b,
        nc.sbuf_tensor(f"{tag}_h", [128, nhh * TB], BF16) as hb,
        nc.sbuf_tensor(f"{tag}_or", [128, NCH * TB], F32) as orow,
        nc.Block() as blk,
    ):
        bS = p.base("dS")
        bM = p.base("mm")
        bQ = p.base("aq")
        bE = p.base("ev")
        bA = p.base("dP2")
        per_hf = NCH + nhh
        nu = 2 * ntb

        def evac_wait(eng, k):
            if k < 0:
                return
            u, r = divmod(k, gph)
            if r < nhh:
                p.wge(eng, "aq", bQ + u * nhh + r + 1)
            else:
                p.wge(eng, "ev", bE + u * NCH + (r - nhh) + 1)

        @blk.sync
        def _(sy):
            for u in range(nu):
                tb, hf = divmod(u, 2)
                if u >= 1:
                    p.wge(sy, "mm", bM + gph * (u - 1) + nhh)
                for c in range(NCH):
                    p.dma(sy, wb[:, c * HH:(c + 1) * HH],
                          w1[c * 128:(c + 1) * 128,
                             hf * HH:(hf + 1) * HH], "dS")
                if u >= 1:
                    p.wge(sy, "mm", bM + gph * u)
                for c in range(nhh):
                    p.dma(sy, w2b[:, c * DM:(c + 1) * DM],
                          w2[hf * HH + c * 128:hf * HH + (c + 1) * 128,
                             :], "dS")

        @blk.tensor
        def _(t):
            for u in range(nu):
                tb, hf = divmod(u, 2)
                t0 = tb * TB
                p.wge(t, "dS", bS + 16 * (per_hf * u + NCH))
                for j in range(nhh):
                    ps = p.ps[j % 2]
                    k = gph * u + j
                    evac_wait(t, k - 2)
                    for c in range(NCH):
                        mm = t.matmul(ps[:, :],
                                      wb[:, c * HH + j * 128:
                                         c * HH + (j + 1) * 128],
                                      xs[c][:, t0:t0 + TB],
                                      start=(c == 0), stop=(c == NCH - 1))
                    p.cinc(mm, "mm", 1)
                p.wge(t, "dS", bS + 16 * per_hf * (u + 1))
                p.wge(t, "aq", bQ + (u + 1) * nhh)
                for j in range(NCH):
                    ps = p.ps[j % 2]
                    k = gph * u + nhh + j
                    evac_wait(t, k - 2)
                    for c in range(nhh):
                        mm = t.matmul(ps[:, :],
                                      w2b[:, c * DM + j * 128:
                                          c * DM + (j + 1) * 128],
                                      hb[:, c * TB:(c + 1) * TB],
                                      start=(c == 0), stop=(c == nhh - 1))
                    p.cinc(mm, "mm", 1)

        @blk.scalar
        def _(s):
            for u in range(nu):
                tb, hf = divmod(u, 2)
                for j in range(nhh):
                    ps = p.ps[j % 2]
                    k = gph * u + j
                    p.wge(s, "mm", bM + k + 1)
                    kw = {}
                    if b1c is not None:
                        kw["bias"] = b1c[:, hf * nhh + j:hf * nhh + j + 1]
                    if scale1 is not None:
                        kw["scale"] = scale1[:, hf * nhh + j:hf * nhh + j + 1]
                    a = s.activation(hb[:, j * TB:(j + 1) * TB], ps[:, :],
                                     afun, **kw)
                    p.cinc(a, "aq", 1)

        @blk.vector
        def _(v):
            for u in range(nu):
                tb, hf = divmod(u, 2)
                for j in range(NCH):
                    ps = p.ps[j % 2]
                    k = gph * u + nhh + j
                    p.wge(v, "mm", bM + k + 1)
                    if hf == 0:
                        if tb >= 1 and j == 0:
                            p.wge(v, "dP2", bA + 16 * NCH * tb)
                        if b2c is not None:
                            cp = v.tensor_scalar_add(
                                orow[:, j * TB:(j + 1) * TB], ps[:, :],
                                b2c[:, j:j + 1])
                        else:
                            cp = v.tensor_copy(orow[:, j * TB:(j + 1) * TB],
                                               ps[:, :])
                    else:
                        cp = v.tensor_add(orow[:, j * TB:(j + 1) * TB],
                                          orow[:, j * TB:(j + 1) * TB],
                                          ps[:, :])
                    p.cinc(cp, "ev", 1)

        @blk.gpsimd
        def _(g):
            for tb in range(ntb):
                t0 = tb * TB
                p.wge(g, "ev", bE + NCH * 2 * (tb + 1))
                for j in range(NCH):
                    p.dma(g, x_res[j * 128:(j + 1) * 128, t0:t0 + TB],
                          orow[:, j * TB:(j + 1) * TB], "dP2",
                          accum_op=ALU.add)
            p.wait_all(g, "dP2")


def emit_l0_ffn(p):
    nc = p.nc
    b1c = _sbuf(nc, "l0_b1c", [128, 32], F32)
    b2c = _sbuf(nc, "l0_b2c", [128, 8], F32)
    emit_pervec(p, "L0_ffn_b1", 4096, b1c)
    emit_pervec(p, "L0_ffn_b2", DM, b2c)
    emit_quantize_ternary(p, "l0q1", "l0_W1T", DM, 4096, "l0_wq1_hbm",
                          eps=1e-5)
    emit_quantize_ternary(p, "l0q2", "l0_W2T", 4096, DM, "l0_wq2_hbm",
                          eps=1e-5)
    emit_ffn2(p, "l0f", "l0_wq1_hbm", "l0_wq2_hbm", 4096, b1c, b2c, "relu")





def emit_l2_attn(p):
    nc = p.nc
    xn = [p.sb[f"xn{c}"] for c in range(NCH)]
    o_scr = p.dram["o_scr"].ap()
    from contextlib import ExitStack
    with ExitStack() as stck:
        en = lambda *a: stck.enter_context(nc.sbuf_tensor(*a))
        wq_sb = en("l2_wq", [128, NCH * 1024], BF16)
        wk_sb = en("l2_wk", [128, NCH * 1024], BF16)
        wv_sb = en("l2_wv", [128, NCH * 1024], BF16)
        bqs = en("l2_bqs", [128, NCH], F32)
        bks = en("l2_bks", [128, NCH], F32)
        bvs = en("l2_bvs", [128, NCH], F32)
        q2 = en("l2_q2", [128, SEQ], BF16)
        k2 = en("l2_k2", [128, SEQ], BF16)
        v_sb = en("l2_v", [128, 16 * 1024], BF16)
        pt = en("l2_pt", [128, 16 * 512], BF16)
        rrow = en("l2_rr", [1, 512], F32)
        rb = en("l2_rb", [64, 512], F32)
        ohb = en("l2_oh", [128, SEQ], BF16)

        bvrow = en("l2_bvrow", [1, 1024], F32)
        bvt = en("l2_bvt", [128, 1024], F32)
        emit_pervec(p, "L2_attn_bq", DM, bqs)
        emit_pervec(p, "L2_attn_bk", DM, bks)

        with nc.Block() as blk:
            bP_ = p.base("dP")
            bMM = p.base("mm")
            nload = NCH * 3 + 1

            @blk.gpsimd
            def _(g):
                for nm, sbuf in (("l2_WqT", wq_sb), ("l2_WkT", wk_sb),
                                 ("l2_WvT", wv_sb)):
                    w = p.dram[nm].ap()
                    for c in range(NCH):
                        p.dma(g, sbuf[:, c * 1024:(c + 1) * 1024],
                              w[c * 128:(c + 1) * 128, :], "dP")
                bv = p.dram["L2_attn_bv"].ap()
                p.dma(g, bvrow[:, :], bv.rearrange("(o a) -> o a", o=1), "dP")

            @blk.tensor
            def _(t):
                p.wge(t, "dP", bP_ + 16 * nload)
                for half in range(2):
                    mm = t.matmul(p.ps[half][:, :],
                                  p.sb["ones_row_f32"][:, :],
                                  bvrow[0:1, half * 512:(half + 1) * 512],
                                  start=True, stop=True)
                    p.cinc(mm, "mm", 1)

            @blk.vector
            def _(v):
                v.tensor_scalar_mul(bqs[:, :], bqs[:, :], 0.125)
                for half in range(2):
                    p.wge(v, "mm", bMM + half + 1)
                    v.tensor_copy(bvt[:, half * 512:(half + 1) * 512],
                                  p.ps[half][:, :])

        for s in range(NSEQ):
            with nc.Block() as blk:
                bMM = p.base("mm")
                bEV = p.base("ev")
                ps0, ps1 = p.ps[0], p.ps[1]

                @blk.tensor
                def _(t):
                    k = 0
                    for tt in range(SEQ // 128):
                        for hf in range(2):
                            ps = (ps0, ps1)[k % 2]
                            if k >= 2:
                                p.wge(t, "ev", bEV + k - 1)
                            for c in range(NCH):
                                mm = t.matmul(
                                    ps[:, :],
                                    xn[c][:, s * SEQ + tt * 128:
                                          s * SEQ + (tt + 1) * 128],
                                    wv_sb[:, c * 1024 + hf * 512:
                                          c * 1024 + (hf + 1) * 512],
                                    start=(c == 0), stop=(c == NCH - 1))
                            p.cinc(mm, "mm", 1)
                            k += 1

                @blk.vector
                def _(v):
                    k = 0
                    for tt in range(SEQ // 128):
                        for hf in range(2):
                            ps = (ps0, ps1)[k % 2]
                            p.wge(v, "mm", bMM + k + 1)
                            cp = v.tensor_add(
                                v_sb[:, tt * 1024 + hf * 512:
                                     tt * 1024 + (hf + 1) * 512], ps[:, :],
                                bvt[:, hf * 512:(hf + 1) * 512])
                            p.cinc(cp, "ev", 1)
                            k += 1

            for hp in range(8):
                with nc.Block() as blk:
                    bMM = p.base("mm")
                    bEV = p.base("ev")
                    ps0, ps1 = p.ps[0], p.ps[1]

                    @blk.tensor
                    def _(t):
                        k = 0
                        for (w_sb, dst, bcol) in ((wq_sb, q2, bqs),
                                                  (wk_sb, k2, bks)):
                            for js in range(SEQ // 512):
                                ps = (ps0, ps1)[k % 2]
                                if k >= 2:
                                    p.wge(t, "ev", bEV + k - 1)
                                for c in range(NCH):
                                    mm = t.matmul(
                                        ps[:, :],
                                        w_sb[:, c * 1024 + hp * 128:
                                             c * 1024 + (hp + 1) * 128],
                                        xn[c][:, s * SEQ + js * 512:
                                              s * SEQ + (js + 1) * 512],
                                        start=(c == 0), stop=(c == NCH - 1))
                                p.cinc(mm, "mm", 1)
                                k += 1

                    @blk.vector
                    def _(v):
                        k = 0
                        for (w_sb, dst, bcol) in ((wq_sb, q2, bqs),
                                                  (wk_sb, k2, bks)):
                            for js in range(SEQ // 512):
                                ps = (ps0, ps1)[k % 2]
                                p.wge(v, "mm", bMM + k + 1)
                                cp = v.tensor_scalar_add(
                                    dst[:, js * 512:(js + 1) * 512],
                                    ps[:, :], bcol[:, hp:hp + 1])
                                p.cinc(cp, "ev", 1)
                                k += 1

                def get_k(s_, h, kt):
                    return k2[64 * h:64 * h + 64, kt * 128:(kt + 1) * 128]

                def get_q(s_, h, qb):
                    return q2[64 * h:64 * h + 64, qb * 512:(qb + 1) * 512]

                def get_v(s_, h, kt, hp=hp):
                    gh = 2 * hp + h
                    return v_sb[:, kt * 1024 + gh * 64:
                                kt * 1024 + gh * 64 + 64]

                def o_slc(par, lo, hi):
                    return ohb[64 * par:64 * par + 64, lo:hi]

                def o_store(g, s_, h, buf, hp=hp):
                    gh = 2 * hp + h
                    p.dma(g, o_scr[64 * gh:64 * gh + 64,
                                   s_ * SEQ:(s_ + 1) * SEQ], buf, "dP")

                emit_attn_core(p, f"l2a{s}_{hp}", [s], 2, get_k, get_q,
                               get_v, o_slc, o_store, None, 0, pt, rrow, rb)

    nc2 = p.nc
    bos = _sbuf(nc2, "l2_bos", [128, NCH], F32)
    emit_pervec(p, "L2_attn_bo", DM, bos)
    emit_oproj(p, "l2o", "l2_WoT", "o_scr", bias_col=bos)


def emit_bv_fix(p):
    pass


def emit_linear_stream(p, tag, in_name, w_name, out_name, DIN, DOUT, act,
                       scale_col=None, accum=False, mult_bcast=None,
                       post_mult=None):
    nc = p.nc
    w = p.dram[w_name].ap()
    xin = p.dram[in_name].ap()
    x_res = p.dram["x_res"].ap()
    nin = DIN // 128
    QC = min(DOUT, 1024)
    nq = DOUT // QC
    nj = QC // 128
    nts = T // 512
    afun = {None: AF.Copy, "relu": AF.Relu, "gelu": AF.Gelu,
            "gelu_tanh": AF.Gelu_apprx_tanh}[act]
    from contextlib import ExitStack
    with ExitStack() as stck:
        en = lambda *a: stck.enter_context(nc.sbuf_tensor(*a))
        wqb = en(f"{tag}_w", [128, nin * QC], BF16)
        ia0 = en(f"{tag}_i0", [128, nin * 512], BF16)
        ob = en(f"{tag}_o", [128, nj * 512], F32 if accum else BF16)
        with nc.Block() as blk:
            bS = p.base("dS")
            bL = p.base("dP")
            bM = p.base("mm")
            bQ = p.base("aq")
            bA = p.base("dP2")

            @blk.sync
            def _(sy):
                for q in range(nq):
                    if q >= 1:
                        p.wge(sy, "mm", bM + nj * nts * q)
                    for c in range(nin):
                        p.dma(sy, wqb[:, c * QC:(c + 1) * QC],
                              w[c * 128:(c + 1) * 128,
                                q * QC:(q + 1) * QC], "dS")

            @blk.gpsimd
            def _(g):
                nu = nq * nts
                for u in range(nu):
                    q, ts = divmod(u, nts)
                    if u >= 1:
                        p.wge(g, "mm", bM + nj * u)
                    for c in range(nin):
                        p.dma(g, ia0[:, c * 512:(c + 1) * 512],
                              xin[c * 128:(c + 1) * 128,
                                  ts * 512:(ts + 1) * 512], "dP")
                    p.wge(g, "aq", bQ + nj * (u + 1))
                    for j in range(nj):
                        row = q * QC + j * 128
                        if accum:
                            p.dma(g, x_res[row:row + 128,
                                           ts * 512:(ts + 1) * 512],
                                  ob[:, j * 512:(j + 1) * 512], "dP2",
                                  accum_op=ALU.add)
                        else:
                            p.dma(g, p.dram[out_name].ap()[
                                row:row + 128, ts * 512:(ts + 1) * 512],
                                ob[:, j * 512:(j + 1) * 512], "dP2")
                p.wait_all(g, "dP2")

            @blk.tensor
            def _(t):
                for q in range(nq):
                    p.wge(t, "dS", bS + 16 * nin * (q + 1))
                    for ts in range(nts):
                        u = q * nts + ts
                        ia = ia0
                        p.wge(t, "dP", bL + 16 * nin * (u + 1))
                        for j in range(nj):
                            ps = p.ps[j % 2]
                            k = u * nj + j
                            if k >= 2:
                                p.wge(t, "aq", bQ + k - 1)
                            for c in range(nin):
                                mm = t.matmul(ps[:, :],
                                              wqb[:, c * QC + j * 128:
                                                  c * QC + (j + 1) * 128],
                                              ia[:, c * 512:(c + 1) * 512],
                                              start=(c == 0),
                                              stop=(c == nin - 1))
                            p.cinc(mm, "mm", 1)

            @blk.scalar
            def _(s):
                for q in range(nq):
                    for ts in range(nts):
                        u = q * nts + ts
                        for j in range(nj):
                            ps = p.ps[j % 2]
                            k = u * nj + j
                            p.wge(s, "mm", bM + k + 1)
                            if j == 0 and u >= 1:
                                p.wge(s, "dP2", bA + 16 * nj * u)
                            kw = {}
                            if scale_col is not None:
                                kw["scale"] = scale_col[:, q * nj + j:
                                                        q * nj + j + 1]
                            a = s.activation(ob[:, j * 512:(j + 1) * 512],
                                             ps[:, :], afun, **kw)
                            p.cinc(a, "aq", 1)


def emit_cast_pass(p, src_name, dst_name, nrows, ncols):
    nc = p.nc
    src = p.dram[src_name].ap()
    dst = p.dram[dst_name].ap()
    rows_per = max(128, min(nrows, (8 << 20) // (ncols * 4)))
    with nc.Block() as blk:
        @blk.gpsimd
        def _(g):
            r = 0
            while r < nrows:
                n = min(rows_per, nrows - r)
                p.dma(g, dst[r:r + n, :], src[r:r + n, :], "dP")
                r += n
            p.wait_all(g, "dP")


L2FFN_PARTS = {"prep", "stageA", "h1", "h2", "w3"}


def emit_l2_ffn(p):
    nc = p.nc
    xn = [p.sb[f"xn{c}"] for c in range(NCH)]
    x_res = p.dram["x_res"].ap()
    s1c = _sbuf(nc, "l2_s1c", [128, 32], F32)
    s2c = _sbuf(nc, "l2_s2c", [128, 32], F32)
    h1c = _sbuf(nc, "l2_h1c", [128, 32], F32)
    h2c = _sbuf(nc, "l2_h2c", [128, 32], F32)
    s3c = _sbuf(nc, "l2_s3c", [128, 8], F32)
    emit_pervec(p, "L2_ffn_w1_s", 4096, s1c)
    emit_pervec(p, "L2_ffn_w2_s", 4096, s2c)
    emit_pervec(p, "L2_ffn_h1_s", 4096, h1c)
    emit_pervec(p, "L2_ffn_h2_s", 4096, h2c)
    emit_pervec(p, "L2_ffn_w3_s", DM, s3c)
    if "prep" in L2FFN_PARTS:
        for (a, b, r, c) in (("l2_w1T", "l2_w1q", DM, 4096),
                             ("l2_w2T", "l2_w2q", DM, 4096),
                             ("l2_h1T", "l2_h1q", 4096, 4096),
                             ("l2_h2T", "l2_h2q", 4096, 4096),
                             ("l2_w3T", "l2_w3q", 4096, DM)):
            emit_cast_pass(p, a, b, r, c)

    if "stageA" not in L2FFN_PARTS:
        return
    w1 = p.dram["l2_w1q"].ap()
    w2 = p.dram["l2_w2q"].ap()
    A = p.dram["l2_A"].ap()
    from contextlib import ExitStack
    with ExitStack() as stck:
        en = lambda *a: stck.enter_context(nc.sbuf_tensor(*a))
        w1h = en("l2a_w1", [128, NCH * 2048], BF16)
        w2h = en("l2a_w2", [128, NCH * 2048], BF16)
        hb = en("l2a_h", [128, 512], BF16)
        arow = en("l2a_a", [128, 16 * 512], BF16)
        psL0 = stck.enter_context(nc.psum_tensor("l2a_p0", [128, 512], F32))
        psL1 = stck.enter_context(nc.psum_tensor("l2a_p1", [128, 512], F32))
        with nc.Block() as blk:
            bS = p.base("dS")
            bM = p.base("mm")
            bQ = p.base("aq")
            bE = p.base("ev")
            bA = p.base("dP2")
            nts = T // 512
            nj = 16

            @blk.sync
            def _(sy):
                for hf in range(2):
                    if hf >= 1:
                        p.wge(sy, "mm", bM + nj * nts * hf)
                    for c in range(NCH):
                        p.dma(sy, w1h[:, c * 2048:(c + 1) * 2048],
                              w1[c * 128:(c + 1) * 128,
                                 hf * 2048:(hf + 1) * 2048], "dS")
                        p.dma(sy, w2h[:, c * 2048:(c + 1) * 2048],
                              w2[c * 128:(c + 1) * 128,
                                 hf * 2048:(hf + 1) * 2048], "dS")

            @blk.tensor
            def _(t):
                for hf in range(2):
                    p.wge(t, "dS", bS + 16 * 2 * NCH * (hf + 1))
                    for ts in range(nts):
                        for j in range(nj):
                            u = (hf * nts + ts) * nj + j
                            pa = (p.ps[0], psL0)[j % 2]
                            pb = (p.ps[1], psL1)[j % 2]
                            if u >= 2:
                                p.wge(t, "ev", bE + u - 1)
                            for c in range(NCH):
                                t.matmul(pa[:, :],
                                         w1h[:, c * 2048 + j * 128:
                                             c * 2048 + (j + 1) * 128],
                                         xn[c][:, ts * 512:(ts + 1) * 512],
                                         start=(c == 0), stop=(c == NCH - 1))
                            for c in range(NCH):
                                mm = t.matmul(pb[:, :],
                                              w2h[:, c * 2048 + j * 128:
                                                  c * 2048 + (j + 1) * 128],
                                              xn[c][:, ts * 512:(ts + 1) * 512],
                                              start=(c == 0),
                                              stop=(c == NCH - 1))
                            p.cinc(mm, "mm", 1)

            @blk.scalar
            def _(s):
                for hf in range(2):
                    for ts in range(nts):
                        for j in range(nj):
                            u = (hf * nts + ts) * nj + j
                            pa = (p.ps[0], psL0)[j % 2]
                            p.wge(s, "mm", bM + u + 1)
                            if u >= 1:
                                p.wge(s, "ev", bE + u)
                            a = s.activation(hb[:, :], pa[:, :], AF.Silu,
                                             scale=s1c[:, hf * 16 + j:
                                                       hf * 16 + j + 1])
                            p.cinc(a, "aq", 1)

            @blk.vector
            def _(v):
                for hf in range(2):
                    for ts in range(nts):
                        for j in range(nj):
                            u = (hf * nts + ts) * nj + j
                            pb = (p.ps[1], psL1)[j % 2]
                            p.wge(v, "aq", bQ + u + 1)
                            if j == 0 and (hf * nts + ts) >= 1:
                                p.wge(v, "dP2", bA + 16 * nj * (hf * nts + ts))
                            st = v.scalar_tensor_tensor(
                                arow[:, j * 512:(j + 1) * 512], pb[:, :],
                                s2c[:, hf * 16 + j:hf * 16 + j + 1],
                                hb[:, :], ALU.mult, ALU.mult)
                            p.cinc(st, "ev", 1)

            @blk.gpsimd
            def _(g):
                for hf in range(2):
                    for ts in range(nts):
                        tsu = hf * nts + ts
                        p.wge(g, "ev", bE + nj * (tsu + 1))
                        for j in range(nj):
                            row = hf * 2048 + j * 128
                            p.dma(g, A[row:row + 128,
                                       ts * 512:(ts + 1) * 512],
                                  arow[:, j * 512:(j + 1) * 512], "dP2")
                p.wait_all(g, "dP2")

    if "h1" in L2FFN_PARTS:
        emit_linear_stream(p, "l2h1", "l2_A", "l2_h1q", "l2_B", 4096, 4096,
                           "gelu", scale_col=h1c)
    if "h2" in L2FFN_PARTS:
        emit_linear_stream(p, "l2h2", "l2_B", "l2_h2q", "l2_A", 4096, 4096,
                           "gelu", scale_col=h2c)
    if "w3" in L2FFN_PARTS:
        emit_linear_stream(p, "l2w3", "l2_A", "l2_w3q", None, 4096, DM,
                           None, scale_col=s3c, accum=True)



def emit_l3_attn(p):
    nc = p.nc
    xn = [p.sb[f"xn{c}"] for c in range(NCH)]
    o_scr = p.dram["o_scr"].ap()
    H = 4
    DH = 256
    M = 64
    nkt = SEQ // 128
    from contextlib import ExitStack
    with ExitStack() as stck:
        en = lambda *a: stck.enter_context(nc.sbuf_tensor(*a))
        cos_sb = en("l3_cos", [128, SEQ], BF16)
        sin_sb = en("l3_sin", [128, SEQ], BF16)
        skw1 = en("l3_skw1", [128, NCH * 16], BF16)
        svw1 = en("l3_svw1", [128, NCH * 16], BF16)
        skw2 = en("l3_skw2", [16, DH], BF16)
        svw2 = en("l3_svw2", [16, DH], BF16)
        skb = en("l3_skb", [64, 4], F32)
        svb = en("l3_svb", [64, 4], F32)
        gnw = en("l3_gnw", [128, 2], F32)
        u1 = en("l3_u1", [16, T], BF16)
        u2 = en("l3_u2", [16, T], BF16)
        v_sb = en("l3_v", [128, 16 * 1024], BF16)
        wv_fm = en("l3_wv", [64, SEQ], BF16)
        zkr = en("l3_zkr", [64, SEQ], BF16)
        zvr = en("l3_zvr", [64, SEQ], BF16)

        wk_tm = en("l3_wktm", [128, nkt * M], BF16)
        wql = en("l3_wql", [128, NCH * DH], BF16)
        wkl = en("l3_wkl", [128, NCH * DH], BF16)
        wgl = en("l3_wgl", [128, NCH * DH], BF16)
        q_sb = en("l3_q", [128, 2 * SEQ], BF16)
        k_sb = en("l3_k", [128, 2 * SEQ], BF16)
        g_sb = en("l3_g", [128, 2 * SEQ], BF16)
        sgq = en("l3_sgq", [128, 2 * 512], BF16)
        tA = en("l3_tA", [128, 512], BF16)
        tB = en("l3_tB", [128, 512], BF16)
        tC = en("l3_tC", [64, 512], F32)
        st = en("l3_st", [128, nkt * 512], BF16)
        wk_fm = st
        pzb = en("l3_pz", [64, 512], BF16)
        lgb = en("l3_lg", [64, 512], BF16)
        rsrow = en("l3_rs", [1, 512], F32)
        onb = en("l3_on", [128, 2 * 512], BF16)
        ofb = onb
        sqg = tA
        grow = tC

        emit_pervec2(p, "L3_attn_sk_b2", 4, 64, skb)
        emit_pervec2(p, "L3_attn_sv_b2", 4, 64, svb)
        emit_pervec(p, "L3_attn_gnorm_w", DH, gnw)

        with nc.Block() as blk:
            bP_ = p.base("dP")
            bMM = p.base("mm")
            bEV = p.base("ev")
            nload = NCH * 2 + 4

            @blk.gpsimd
            def _(g):
                w1 = p.dram["L3_attn_sk_w1"].ap()
                w1v = p.dram["L3_attn_sv_w1"].ap()
                for c in range(NCH):
                    p.dma(g, skw1[:, c * 16:(c + 1) * 16],
                          w1[c * 128:(c + 1) * 128, :], "dP")
                    p.dma(g, svw1[:, c * 16:(c + 1) * 16],
                          w1v[c * 128:(c + 1) * 128, :], "dP")
                p.dma(g, skw2[:, :], p.dram["L3_attn_sk_w2"].ap(), "dP")
                p.dma(g, svw2[:, :], p.dram["L3_attn_sv_w2"].ap(), "dP")
                p.dma(g, cos_sb[:, :], p.dram["rope_cos"].ap(), "dP")
                p.dma(g, sin_sb[:, :], p.dram["rope_sin"].ap(), "dP")

            @blk.tensor
            def _(t):
                p.wge(t, "dP", bP_ + 16 * nload)
                for j in range(T // 512):
                    for sel, w1s in ((0, skw1), (1, svw1)):
                        k = j * 2 + sel
                        ps = p.ps[k % 2]
                        if k >= 2:
                            p.wge(t, "ev", bEV + k - 1)
                        for c in range(NCH):
                            mm = t.matmul(ps[0:16, :],
                                          w1s[:, c * 16:(c + 1) * 16],
                                          xn[c][:, j * 512:(j + 1) * 512],
                                          start=(c == 0), stop=(c == NCH - 1))
                        p.cinc(mm, "mm", 1)

            @blk.vector
            def _(v):
                for j in range(T // 512):
                    for sel, dst in ((0, u1), (1, u2)):
                        k = j * 2 + sel
                        ps = p.ps[k % 2]
                        p.wge(v, "mm", bMM + k + 1)
                        cp = v.tensor_copy(dst[:, j * 512:(j + 1) * 512],
                                           ps[0:16, :])
                        p.cinc(cp, "ev", 1)

        for s in range(NSEQ):
            for vq in range(4):
                with nc.Block() as blk:
                    bMM = p.base("mm")
                    bEV = p.base("ev")
                    bP_ = p.base("dP")

                    @blk.gpsimd
                    def _(g):
                        wv = p.dram["L3_attn_Wv"].ap()
                        for c in range(NCH):
                            p.dma(g, wql[:, c * 256:(c + 1) * 256],
                                  wv[c * 128:(c + 1) * 128,
                                     vq * 256:(vq + 1) * 256], "dP")

                    @blk.tensor
                    def _(t):
                        p.wge(t, "dP", bP_ + 16 * NCH)
                        for tt in range(nkt):
                            ps = p.ps[tt % 2]
                            if tt >= 2:
                                p.wge(t, "ev", bEV + tt - 1)
                            for c in range(NCH):
                                mm = t.matmul(
                                    ps[:, 0:256],
                                    xn[c][:, s * SEQ + tt * 128:
                                          s * SEQ + (tt + 1) * 128],
                                    wql[:, c * 256:(c + 1) * 256],
                                    start=(c == 0), stop=(c == NCH - 1))
                            p.cinc(mm, "mm", 1)

                    @blk.vector
                    def _(v):
                        for tt in range(nkt):
                            ps = p.ps[tt % 2]
                            p.wge(v, "mm", bMM + tt + 1)
                            cp = v.tensor_copy(
                                v_sb[:, tt * 1024 + vq * 256:
                                     tt * 1024 + (vq + 1) * 256],
                                ps[:, 0:256])
                            p.cinc(cp, "ev", 1)

            for h in range(H):
                _l3_head(p, s, h, dict(
                    xn=xn, o_scr=o_scr, v_sb=v_sb, wk_fm=wk_fm, wv_fm=wv_fm,
                    zkr=zkr, zvr=zvr, wk_tm=wk_tm, wql=wql, wkl=wkl, wgl=wgl,
                    q_sb=q_sb, k_sb=k_sb, g_sb=g_sb, sgq=sgq, tA=tA,
                    tB=tB, tC=tC, st=st, pzb=pzb, lgb=lgb, rsrow=rsrow, onb=onb,
                    ofb=ofb, sqg=sqg, grow=grow, cos_sb=cos_sb, sin_sb=sin_sb,
                    gnw=gnw, u1=u1, u2=u2, skw2=skw2, svw2=svw2, skb=skb,
                    svb=svb, nkt=nkt, M=M, DH=DH))

    emit_oproj(p, "l3o", "L3_attn_Wo", "o_scr", bias_col=None)


def _l3_head(p, s, h, c):
    nc = p.nc
    xn = c["xn"]
    nkt = c["nkt"]
    DH = c["DH"]
    M = c["M"]
    ccq, b64 = divmod(h, 2)
    base = 64 * b64

    with nc.Block() as blk:
        bMM = p.base("mm")
        bAQ = p.base("aq")
        bX = p.base("x1")
        tC = c["tC"]

        @blk.tensor
        def _(t):
            k = 0
            for sel, u_src, w2s in ((0, c["u1"], c["skw2"]),
                                    (1, c["u2"], c["svw2"])):
                for js in range(SEQ // 512):
                    ps = p.ps[k % 2]
                    if k >= 2:
                        p.wge(t, "aq", bAQ + k - 1)
                    mm = t.matmul(ps[0:64, :], w2s[:, h * 64:(h + 1) * 64],
                                  u_src[:, s * SEQ + js * 512:
                                        s * SEQ + (js + 1) * 512],
                                  start=True, stop=True)
                    p.cinc(mm, "mm", 1)
                    k += 1

        @blk.vector
        def _(v):
            k = 0
            for sel, bcol in ((0, c["skb"]), (1, c["svb"])):
                for js in range(SEQ // 512):
                    ps = p.ps[k % 2]
                    p.wge(v, "mm", bMM + k + 1)
                    if k >= 1:
                        p.wge(v, "aq", bAQ + k)
                    st_ = v.tensor_scalar(tC[0:64, :], ps[0:64, :],
                                          bcol[:, h:h + 1], 32.0,
                                          ALU.add, ALU.min)
                    p.cinc(st_, "x1", 1)
                    p.wge(v, "x1", p.val("x1"))
                    st_ = v.tensor_scalar_max(tC[0:64, :], tC[0:64, :],
                                              -32.0)
                    p.cinc(st_, "x1", 1)
                    k += 1

        @blk.scalar
        def _(sc):
            k = 0
            for sel, dst in ((0, c["wk_fm"]), (1, c["wv_fm"])):
                for js in range(SEQ // 512):
                    p.wge(sc, "x1", bX + 2 * (k + 1))
                    a = sc.activation(dst[0:64, js * 512:(js + 1) * 512],
                                      tC[0:64, :], AF.Exp)
                    p.cinc(a, "aq", 1)
                    k += 1

    with nc.Block() as blk:
        @blk.vector
        def _(v):
            for srcb, dst in ((c["wk_fm"], c["zkr"]), (c["wv_fm"], c["zvr"])):
                sc_ = v.tensor_tensor_scan(dst[0:64, :],
                                           srcb[0:64, 0:SEQ],
                                           srcb[0:64, 0:SEQ], 0.0,
                                           ALU.add, ALU.max)
                p.cinc(sc_, "x1", 1)
                p.wge(v, "x1", p.val("x1"))
                with nc.allow_low_precision(reason="bf16 cumsum recip"):
                    cp = v.reciprocal(dst[0:64, :], dst[0:64, :])
                p.cinc(cp, "x1", 1)
                p.wge(v, "x1", p.val("x1"))

    with (nc.psum_tensor(f"l3_ptb0_{s}_{h}", [128, 64], BF16) as pb0,
          nc.psum_tensor(f"l3_ptb1_{s}_{h}", [128, 64], BF16) as pb1,
          nc.Block() as blk):
        bMM = p.base("mm")
        bEV = p.base("ev")

        @blk.tensor
        def _(t):
            for kt in range(nkt):
                pb = (pb0, pb1)[kt % 2]
                if kt >= 2:
                    p.wge(t, "ev", bEV + kt - 1)
                mm = t.transpose(pb[:, :],
                                 c["wk_fm"][0:64, kt * 128:(kt + 1) * 128],
                                 p.sb["ident_bf16"][0:64, 0:64])
                p.cinc(mm, "mm", 1)

        @blk.vector
        def _(v):
            for kt in range(nkt):
                pb = (pb0, pb1)[kt % 2]
                p.wge(v, "mm", bMM + kt + 1)
                cp = v.tensor_copy(c["wk_tm"][:, kt * M:(kt + 1) * M],
                                   pb[:, :])
                p.cinc(cp, "ev", 1)

    with nc.Block() as blk:
        bMM = p.base("mm")
        bEV = p.base("ev")
        bP_ = p.base("dP")
        nsl = SEQ // 512

        @blk.gpsimd
        def _(g):
            for nm, dst in (("L3_attn_Wq", c["wql"]), ("L3_attn_Wk", c["wkl"]),
                            ("L3_attn_Wg", c["wgl"])):
                w = p.dram[nm].ap()
                for ch in range(NCH):
                    p.dma(g, dst[:, ch * DH:(ch + 1) * DH],
                          w[ch * 128:(ch + 1) * 128,
                            h * DH:(h + 1) * DH], "dP")

        @blk.tensor
        def _(t):
            p.wge(t, "dP", bP_ + 16 * NCH * 3)
            for i, (wsl, dst) in enumerate(((c["wql"], c["q_sb"]),
                                            (c["wkl"], c["k_sb"]))):
                for js in range(nsl):
                    it = i * nsl + js
                    if it >= 1:
                        p.wge(t, "ev", bEV + 6 * it)
                    for cc2 in range(2):
                        ps = p.ps[cc2]
                        for ch in range(NCH):
                            mm = t.matmul(
                                ps[:, :],
                                wsl[:, ch * DH + cc2 * 128:
                                    ch * DH + (cc2 + 1) * 128],
                                xn[ch][:, s * SEQ + js * 512:
                                       s * SEQ + (js + 1) * 512],
                                start=(ch == 0), stop=(ch == NCH - 1))
                    p.cinc(mm, "mm", 1)
            for gi in range(nsl * 2):
                js, cc2 = divmod(gi, 2)
                ps = p.ps[cc2]
                if gi >= 2:
                    p.wge(t, "ev", bEV + 6 * 2 * nsl + gi - 1)
                else:
                    p.wge(t, "ev", bEV + 6 * 2 * nsl)
                for ch in range(NCH):
                    mm = t.matmul(
                        ps[:, :],
                        c["wgl"][:, ch * DH + cc2 * 128:
                                 ch * DH + (cc2 + 1) * 128],
                        xn[ch][:, s * SEQ + js * 512:
                               s * SEQ + (js + 1) * 512],
                        start=(ch == 0), stop=(ch == NCH - 1))
                p.cinc(mm, "mm", 1)

        @blk.vector
        def _(v):
            tA, tB = c["tA"], c["tB"]
            for i, (wsl, dst) in enumerate(((c["wql"], c["q_sb"]),
                                            (c["wkl"], c["k_sb"]))):
                for js in range(nsl):
                    it = i * nsl + js
                    p.wge(v, "mm", bMM + it + 1)
                    co = c["cos_sb"][:, js * 512:(js + 1) * 512]
                    si = c["sin_sb"][:, js * 512:(js + 1) * 512]
                    d0 = dst[:, 0 * SEQ + js * 512:0 * SEQ + (js + 1) * 512]
                    d1 = dst[:, 1 * SEQ + js * 512:1 * SEQ + (js + 1) * 512]
                    st1 = v.tensor_mul(tA[:, :], p.ps[0][:, :], co)
                    p.cinc(st1, "x1", 1)
                    st1 = v.tensor_mul(tB[:, :], p.ps[1][:, :], si)
                    p.cinc(st1, "x1", 1)
                    p.wge(v, "x1", p.val("x1"))
                    st1 = v.tensor_sub(d0, tA[:, :], tB[:, :])
                    p.cinc(st1, "ev", 1)
                    st1 = v.tensor_mul(tA[:, :], p.ps[1][:, :], co)
                    p.cinc(st1, "x1", 1)
                    st1 = v.tensor_mul(tB[:, :], p.ps[0][:, :], si)
                    p.cinc(st1, "x1", 1)
                    p.wge(v, "x1", p.val("x1"))
                    st1 = v.tensor_add(d1, tA[:, :], tB[:, :])
                    p.cinc(st1, "ev", 5)
            for gi in range(nsl * 2):
                js, cc2 = divmod(gi, 2)
                p.wge(v, "mm", bMM + 2 * nsl + gi + 1)
                dg = c["g_sb"][:, cc2 * SEQ + js * 512:
                               cc2 * SEQ + (js + 1) * 512]
                cp = v.tensor_copy(dg, p.ps[cc2][:, :])
                p.cinc(cp, "ev", 1)


    for qb in range(4):
        _l3_qb(p, s, h, qb, c)


def _l3_qb(p, s, h, qb, c):
    nc = p.nc
    nkt4 = 4 * (qb + 1)
    ccq, b64 = divmod(h, 2)
    base = 64 * b64
    M = c["M"]
    nkt = c["nkt"]
    q0 = qb * 512
    st = c["st"]
    partial = [4 * qb + j for j in range(4)]
    sfx = f"{s}_{h}_{qb}"
    with (nc.psum_tensor(f"l3q_lg{sfx}", [64, 512], F32) as psLg,
          nc.psum_tensor(f"l3q_s1{sfx}", [1, 512], F32) as psS1,
          nc.psum_tensor(f"l3q_b1{sfx}", [128, 512], F32) as psB1,
          nc.psum_tensor(f"l3q_o0{sfx}", [128, 512], F32) as psO0,
          nc.psum_tensor(f"l3q_o1{sfx}", [128, 512], F32) as psO1):

        with nc.Block() as blk:
            bMM = p.base("mm")
            bEV = p.base("ev")
            bPL = p.base("pl")

            @blk.tensor
            def _(t):
                for kt in range(nkt4):
                    ps = p.ps[kt % 2]
                    if kt >= 2:
                        p.wge(t, "ev", bEV + kt - 1)
                    for cc2 in range(2):
                        mm = t.matmul(
                            ps[:, :],
                            c["k_sb"][:, cc2 * SEQ + kt * 128:
                                      cc2 * SEQ + (kt + 1) * 128],
                            c["q_sb"][:, cc2 * SEQ + q0:cc2 * SEQ + q0 + 512],
                            start=(cc2 == 0), stop=(cc2 == 1))
                    p.cinc(mm, "mm", 1)

            @blk.vector
            def _(v):
                for kt in range(nkt4):
                    ps = p.ps[kt % 2]
                    p.wge(v, "mm", bMM + kt + 1)
                    cp = v.tensor_copy(st[:, kt * 512:(kt + 1) * 512],
                                       ps[:, :])
                    p.cinc(cp, "ev", 1)

            @blk.gpsimd
            def _(g):
                for j, kt in enumerate(partial):
                    p.wge(g, "ev", bEV + kt + 1)
                    a = g.affine_select(
                        st[:, kt * 512:(kt + 1) * 512],
                        st[:, kt * 512:(kt + 1) * 512],
                        [[1, 512]], ALU.is_ge, 0.0,
                        base=qb * 512 - kt * 128, channel_multiplier=-1)
                    p.cinc(a, "pl", 1)

        with nc.Block() as blk:
            bM2 = p.base("mm2")
            bE2 = p.base("ev2")
            bAQ = p.base("aq")

            @blk.tensor
            def _(t):
                for i in range(nkt4):
                    mm = t.matmul(psLg[:, :],
                                  c["wk_tm"][:, i * M:(i + 1) * M],
                                  st[:, i * 512:(i + 1) * 512],
                                  start=(i == 0), stop=(i == nkt4 - 1))
                p.cinc(mm, "mm2", 1)
                p.wge(t, "aq", bAQ + 1)
                mm = t.matmul(psS1[0:1, :], p.sb["ones_col_bf16"][0:64, :],
                              c["pzb"][0:64, :],
                              start=True, stop=True)
                p.cinc(mm, "mm2", 1)
                p.wge(t, "ev2", bE2 + 2)
                mm = t.matmul(psB1[:, :], p.sb["ones_row_f32"][:, :],
                              c["rsrow"][0:1, :], start=True, stop=True)
                p.cinc(mm, "mm2", 1)

            @blk.vector
            def _(v):
                p.wge(v, "mm2", bM2 + 1)
                st_ = v.scalar_tensor_tensor(
                    c["lgb"][0:64, :], psLg[:, :], 1.0 / 16.0,
                    c["zkr"][0:64, q0:q0 + 512],
                    ALU.mult, ALU.mult)
                p.cinc(st_, "ev2", 1)
                p.wge(v, "mm2", bM2 + 2)
                cp = v.reciprocal(c["rsrow"][0:1, :], psS1[0:1, :])
                p.cinc(cp, "ev2", 1)
                p.wge(v, "mm2", bM2 + 3)
                cp = v.tensor_mul(c["pzb"][0:64, :],
                                  c["pzb"][0:64, :],
                                  psB1[0:64, :])
                p.cinc(cp, "ev2", 1)
                p.wge(v, "ev2", p.val("ev2"))
                cp = v.tensor_mul(c["pzb"][0:64, :],
                                  c["pzb"][0:64, :],
                                  c["zvr"][0:64, q0:q0 + 512])
                p.cinc(cp, "ev2", 1)

            @blk.scalar
            def _(sc):
                p.wge(sc, "ev2", bE2 + 1)
                a = sc.activation(c["pzb"][0:64, :],
                                  c["lgb"][0:64, :], AF.Exp)
                p.cinc(a, "aq", 1)

        with nc.Block() as blk:
            bMM = p.base("mm")
            bEV = p.base("ev")
            bPL = p.base("pl")

            @blk.tensor
            def _(t):
                for kt in range(nkt4):
                    ps = p.ps[kt % 2]
                    if kt >= 2:
                        p.wge(t, "ev", bEV + kt - 1)
                    mm = t.matmul(ps[:, :],
                                  c["wv_fm"][0:64, kt * 128:(kt + 1) * 128],
                                  c["pzb"][0:64, :],
                                  start=True, stop=True)
                    p.cinc(mm, "mm", 1)
                p.wge(t, "ev", bEV + nkt4)
                p.wge(t, "pl", bPL + 4)
                for cc2 in range(2):
                    pso = (psO0, psO1)[cc2]
                    for i in range(nkt4):
                        mm = t.matmul(
                            pso[:, :],
                            c["v_sb"][:, i * 1024 + 256 * h + 128 * cc2:
                                      i * 1024 + 256 * h + 128 * (cc2 + 1)],
                            st[:, i * 512:(i + 1) * 512],
                            start=(i == 0), stop=(i == nkt4 - 1))
                    p.cinc(mm, "mm", 1)

            @blk.vector
            def _(v):
                for kt in range(nkt4):
                    ps = p.ps[kt % 2]
                    p.wge(v, "mm", bMM + kt + 1)
                    cp = v.tensor_copy(st[:, kt * 512:(kt + 1) * 512],
                                       ps[:, :])
                    p.cinc(cp, "ev", 1)

            @blk.gpsimd
            def _(g):
                for j, kt in enumerate(partial):
                    p.wge(g, "ev", bEV + kt + 1)
                    a = g.affine_select(
                        st[:, kt * 512:(kt + 1) * 512],
                        st[:, kt * 512:(kt + 1) * 512],
                        [[1, 512]], ALU.is_ge, 0.0,
                        base=qb * 512 - kt * 128, channel_multiplier=-1)
                    p.cinc(a, "pl", 1)

        with nc.Block() as blk:
            bM2 = p.base("mm2")
            bE2 = p.base("ev2")
            bAQ = p.base("aq")
            bP_ = p.base("dP")

            @blk.scalar
            def _(sc):
                for cc2 in range(2):
                    pso = (psO0, psO1)[cc2]
                    if cc2 == 1:
                        p.wge(sc, "mm2", bM2 + 1)
                    a = sc.activation(c["sqg"][:, :], pso[:, :], AF.Square)
                    p.cinc(a, "aq", 1)
                for cc2 in range(2):
                    a = sc.activation(
                        c["sgq"][:, cc2 * 512:(cc2 + 1) * 512],
                        c["g_sb"][:, cc2 * SEQ + q0:cc2 * SEQ + q0 + 512],
                        AF.Sigmoid)
                    p.cinc(a, "aq", 1)
                p.wge(sc, "ev2", bE2 + 1)
                a = sc.activation(c["grow"][0:1, :], c["grow"][0:1, :],
                                  AF.Sqrt)
                p.cinc(a, "aq", 1)

            @blk.tensor
            def _(t):
                for cc2 in range(2):
                    p.wge(t, "aq", bAQ + cc2 + 1)
                    mm = t.matmul(psS1[0:1, :], p.sb["ones_col_bf16"][:, :],
                                  c["sqg"][:, :],
                                  start=(cc2 == 0), stop=(cc2 == 1))
                    p.cinc(mm, "mm2", 1)
                p.wge(t, "ev2", bE2 + 2)
                mm = t.matmul(psB1[:, :], p.sb["ones_row_f32"][:, :],
                              c["rsrow"][0:1, :], start=True, stop=True)
                p.cinc(mm, "mm2", 1)

            @blk.vector
            def _(v):
                p.wge(v, "mm2", bM2 + 2)
                st_ = v.tensor_scalar(c["grow"][0:1, :], psS1[0:1, :],
                                      1.0 / 256.0, 1e-5, ALU.mult, ALU.add)
                p.cinc(st_, "ev2", 1)
                p.wge(v, "aq", bAQ + 5)
                cp = v.reciprocal(c["rsrow"][0:1, :], c["grow"][0:1, :])
                p.cinc(cp, "ev2", 1)
                p.wge(v, "mm2", bM2 + 3)
                cp = v.tensor_copy(c["tB"][:, :], psB1[:, :])
                p.cinc(cp, "x1", 1)
                p.wge(v, "x1", p.val("x1"))
                for cc2 in range(2):
                    pso = (psO0, psO1)[cc2]
                    st_ = v.scalar_tensor_tensor(
                        c["onb"][:, cc2 * 512:(cc2 + 1) * 512], pso[:, :],
                        c["gnw"][:, cc2:cc2 + 1], c["tB"][:, :],
                        ALU.mult, ALU.mult)
                    p.cinc(st_, "x1", 1)
                p.wge(v, "x1", p.val("x1"))
                for cc2 in range(2):
                    st_ = v.tensor_mul(
                        c["ofb"][:, cc2 * 512:(cc2 + 1) * 512],
                        c["ofb"][:, cc2 * 512:(cc2 + 1) * 512],
                        c["g_sb"][:, cc2 * SEQ + q0:cc2 * SEQ + q0 + 512])
                    p.cinc(st_, "x1", 1)
                p.wge(v, "x1", p.val("x1"))
                for cc2 in range(2):
                    st_ = v.tensor_mul(
                        c["ofb"][:, cc2 * 512:(cc2 + 1) * 512],
                        c["ofb"][:, cc2 * 512:(cc2 + 1) * 512],
                        c["sgq"][:, cc2 * 512:(cc2 + 1) * 512])
                    p.cinc(st_, "ev2", 1)

            @blk.gpsimd
            def _(g):
                p.wge(g, "ev2", bE2 + 4)
                for cc2 in range(2):
                    p.dma(g, c["o_scr"][256 * h + 128 * cc2:
                                        256 * h + 128 * (cc2 + 1),
                                        s * SEQ + q0:s * SEQ + q0 + 512],
                          c["ofb"][:, cc2 * 512:(cc2 + 1) * 512], "dP")
                p.wait_all(g, "dP")


def emit_l3_ffn(p):
    nc = p.nc
    xn = [p.sb[f"xn{c}"] for c in range(NCH)]
    x_res = p.dram["x_res"].ap()
    emit_quantize_ternary(p, "l3q1", "l3_upT", DM, 4096, "l3_upq", eps=0.0)
    emit_quantize_ternary(p, "l3q2", "l3_downT", 4096, DM, "l3_downq",
                          eps=0.0)

    from contextlib import ExitStack
    with ExitStack() as stck:
        en = lambda *a: stck.enter_context(nc.sbuf_tensor(*a))
        sqs = en("l3f_sqs", [1, T], F32)
        rr1 = en("l3f_rr1", [1, T], F32)
        rb1 = en("l3f_rb1", [128, T], BF16)
        sq0 = en("l3f_sq0", [128, 512], BF16)
        sq1 = en("l3f_sq1", [128, 512], BF16)
        nsl = T // 512
        with nc.Block() as blk:
            bQ = p.base("aq")
            bQC = p.base("mm2")
            bM = p.base("mm")
            bE = p.base("ev")

            @blk.scalar
            def _(s):
                for j in range(nsl):
                    for cch in range(NCH):
                        sq = (sq0, sq1)[cch % 2]
                        k = j * NCH + cch
                        if k >= 2:
                            p.wge(s, "mm2", bQC + k - 1)
                        a = s.activation(sq[:, :],
                                         xn[cch][:, j * 512:(j + 1) * 512],
                                         AF.Square)
                        p.cinc(a, "aq", 1)

            @blk.tensor
            def _(t):
                for j in range(nsl):
                    if j >= 1:
                        p.wge(t, "ev", bE + j)
                    for cch in range(NCH):
                        sq = (sq0, sq1)[cch % 2]
                        k = j * NCH + cch
                        p.wge(t, "aq", bQ + k + 1)
                        mm = t.matmul(p.ps[0][0:1, :],
                                      p.sb["ones_col_bf16"][:, :], sq[:, :],
                                      start=(cch == 0), stop=(cch == NCH - 1))
                        p.cinc(mm, "mm2", 1)

            @blk.vector
            def _(v):
                for j in range(nsl):
                    p.wge(v, "mm2", bQC + (j + 1) * NCH)
                    cp = v.tensor_copy(sqs[0:1, j * 512:(j + 1) * 512],
                                       p.ps[0][0:1, :])
                    p.cinc(cp, "ev", 1)

        with nc.Block() as blk:
            bQ = p.base("aq")
            bE = p.base("ev")
            bM = p.base("mm")

            @blk.vector
            def _(v):
                cp = v.tensor_scalar_mul(rr1[0:1, :], sqs[0:1, :], 1.0 / DM)
                p.cinc(cp, "ev", 1)
                p.wge(v, "aq", bQ + 1)
                st_ = v.tensor_scalar(rr1[0:1, :], rr1[0:1, :], 1.0, 1e-8,
                                      ALU.mult, ALU.add)
                p.cinc(st_, "x1", 1)
                p.wge(v, "x1", p.val("x1"))
                cp = v.reciprocal(rr1[0:1, :], rr1[0:1, :])
                p.cinc(cp, "ev", 1)

            @blk.scalar
            def _(s):
                p.wge(s, "ev", bE + 1)
                a = s.activation(rr1[0:1, :], rr1[0:1, :], AF.Sqrt)
                p.cinc(a, "aq", 1)

        with nc.Block() as blk:
            bM = p.base("mm")
            bE = p.base("ev")

            @blk.tensor
            def _(t):
                for j in range(nsl):
                    ps = p.ps[j % 2]
                    if j >= 2:
                        p.wge(t, "ev", bE + j - 1)
                    mm = t.matmul(ps[:, :], p.sb["ones_row_f32"][:, :],
                                  rr1[0:1, j * 512:(j + 1) * 512],
                                  start=True, stop=True)
                    p.cinc(mm, "mm", 1)

            @blk.vector
            def _(v):
                for j in range(nsl):
                    ps = p.ps[j % 2]
                    p.wge(v, "mm", bM + j + 1)
                    cp = v.tensor_copy(rb1[:, j * 512:(j + 1) * 512],
                                       ps[:, :])
                    p.cinc(cp, "ev", 1)

        with nc.Block() as blk:
            @blk.vector
            def _(v):
                for cch in range(NCH):
                    v.tensor_mul(xn[cch][:, :], xn[cch][:, :], rb1[:, :])

    w_up = p.dram["l3_upq"].ap()
    w_dn = p.dram["l3_downq"].ap()
    with ExitStack() as stck:
        en = lambda *a: stck.enter_context(nc.sbuf_tensor(*a))
        wbuf = en("l3f_w", [128, 16 * 2048], BF16)
        hb = en("l3f_h", [128, 32 * 512], BF16)
        sqh0 = en("l3f_sqh0", [128, 512], BF16)
        sqh1 = en("l3f_sqh1", [128, 512], BF16)
        growh = en("l3f_gr", [1, 512], F32)
        rbh = en("l3f_rbh", [128, 512], F32)
        orow = en("l3f_or", [128, NCH * 512], F32)
        tbuf = en("l3f_tb", [128, 512], F32)
        with nc.Block() as blk:
            bS = p.base("dS")
            bM = p.base("mm")
            bQ = p.base("aq")
            bE = p.base("ev")
            bA = p.base("dP2")
            bM2 = p.base("mm2")
            bM3 = p.base("mm3")
            bE2 = p.base("ev2")
            ntb = T // 512

            def aq_s1(k):
                tb, r = divmod(k, 32)
                return bQ + 65 * tb + r + 1

            @blk.sync
            def _(sy):
                for tb in range(ntb):
                    for hf in range(2):
                        l = tb * 4 + hf
                        if l >= 1:
                            cons = (bM + tb * 48 + 16 if hf == 1
                                    else bM + tb * 48)
                            p.wge(sy, "mm", cons)
                        for cch in range(NCH):
                            p.dma(sy, wbuf[:, cch * 2048:(cch + 1) * 2048],
                                  w_up[cch * 128:(cch + 1) * 128,
                                       hf * 2048:(hf + 1) * 2048], "dS")
                    for hf in range(2):
                        cons = (bM + tb * 48 + 32 if hf == 0
                                else bM + tb * 48 + 40)
                        p.wge(sy, "mm", cons)
                        for r in range(16):
                            p.dma(sy, wbuf[:, r * 1024:(r + 1) * 1024],
                                  w_dn[hf * 2048 + r * 128:
                                       hf * 2048 + (r + 1) * 128, :], "dS")

            @blk.tensor
            def _(t):
                for tb in range(ntb):
                    t0 = tb * 512
                    for hf in range(2):
                        p.wge(t, "dS", bS + 16 * (tb * 48 + (hf + 1) * 8))
                        for j in range(16):
                            ps = p.ps[j % 2]
                            k = tb * 32 + hf * 16 + j
                            if k >= 2:
                                p.wge(t, "aq", aq_s1(k - 2))
                            for cch in range(NCH):
                                mm = t.matmul(
                                    ps[:, :],
                                    wbuf[:, cch * 2048 + j * 128:
                                         cch * 2048 + (j + 1) * 128],
                                    xn[cch][:, t0:t0 + 512],
                                    start=(cch == 0), stop=(cch == NCH - 1))
                            p.cinc(mm, "mm", 1)
                    for j in range(32):
                        p.wge(t, "aq", bQ + 65 * tb + 32 + j + 1)
                        mm = t.matmul(p.ps[0][0:1, :],
                                      p.sb["ones_col_bf16"][:, :],
                                      (sqh0, sqh1)[j % 2][:, :],
                                      start=(j == 0), stop=(j == 31))
                        p.cinc(mm, "mm3", 1)
                    p.wge(t, "ev2", bE2 + 2 * tb + 2)
                    mm = t.matmul(p.ps[1][:, :], p.sb["ones_row_f32"][:, :],
                                  growh[0:1, :], start=True, stop=True)
                    p.cinc(mm, "mm2", 1)
                    p.wge(t, "aq", bQ + 65 * tb + 32)
                    for hf in range(2):
                        p.wge(t, "dS", bS + 16 * (tb * 48 + 16 +
                                                  (hf + 1) * 16))
                        for j in range(NCH):
                            ps = p.ps[j % 2]
                            m = hf * 8 + j
                            if m == 0:
                                p.wge(t, "ev2", bE2 + 2 * tb + 1)
                            elif m == 1:
                                p.wge(t, "ev", bE + 17 * tb + 1)
                            else:
                                p.wge(t, "ev", bE + 17 * tb + 1 + m - 1)
                            for r in range(16):
                                mm = t.matmul(
                                    ps[:, :],
                                    wbuf[:, r * 1024 + j * 128:
                                         r * 1024 + (j + 1) * 128],
                                    hb[:, (hf * 16 + r) * 512:
                                       (hf * 16 + r + 1) * 512],
                                    start=(r == 0), stop=(r == 15))
                            p.cinc(mm, "mm", 1)

            @blk.scalar
            def _(s):
                for tb in range(ntb):
                    for hf in range(2):
                        for j in range(16):
                            p.wge(s, "mm", bM + tb * 48 + hf * 16 + j + 1)
                            a = s.activation(
                                hb[:, (hf * 16 + j) * 512:
                                   (hf * 16 + j + 1) * 512],
                                p.ps[j % 2][:, :], AF.Gelu)
                            p.cinc(a, "aq", 1)
                    for j in range(32):
                        if j >= 2:
                            p.wge(s, "mm3", bM3 + 32 * tb + j - 1)
                        a = s.activation((sqh0, sqh1)[j % 2][:, :],
                                         hb[:, j * 512:(j + 1) * 512],
                                         AF.Square)
                        p.cinc(a, "aq", 1)
                    p.wge(s, "ev2", bE2 + 2 * tb + 1)
                    a = s.activation(growh[0:1, :], growh[0:1, :], AF.Sqrt)
                    p.cinc(a, "aq", 1)

            @blk.vector
            def _(v):
                for tb in range(ntb):
                    p.wge(v, "mm3", bM3 + 32 * (tb + 1))
                    cp = v.tensor_scalar_mul(growh[0:1, :], p.ps[0][0:1, :],
                                             1.0 / 4096.0)
                    p.cinc(cp, "ev2", 1)
                    p.wge(v, "aq", bQ + 65 * (tb + 1))
                    st_ = v.tensor_scalar(growh[0:1, :], growh[0:1, :], 1.0,
                                          1e-8, ALU.mult, ALU.add)
                    p.cinc(st_, "x1", 1)
                    p.wge(v, "x1", p.val("x1"))
                    cp = v.reciprocal(growh[0:1, :], growh[0:1, :])
                    p.cinc(cp, "ev2", 1)
                    p.wge(v, "mm2", bM2 + tb + 1)
                    cp = v.tensor_copy(rbh[:, :], p.ps[1][:, :])
                    p.cinc(cp, "ev", 1)
                    p.wge(v, "ev", bE + 17 * tb + 1)
                    for hf in range(2):
                        for j in range(NCH):
                            ps = p.ps[j % 2]
                            p.wge(v, "mm", bM + tb * 48 + 32 + hf * 8 + j + 1)
                            if hf == 0:
                                if tb >= 1 and j == 0:
                                    p.wge(v, "dP2", bA + 16 * NCH * tb)
                                st_ = v.tensor_mul(
                                    orow[:, j * 512:(j + 1) * 512],
                                    ps[:, :], rbh[:, :])
                                p.cinc(st_, "ev", 1)
                            else:
                                st_ = v.tensor_mul(tbuf[:, :], ps[:, :],
                                                   rbh[:, :])
                                p.cinc(st_, "x1", 1)
                                p.wge(v, "x1", p.val("x1"))
                                st_ = v.tensor_add(
                                    orow[:, j * 512:(j + 1) * 512],
                                    orow[:, j * 512:(j + 1) * 512],
                                    tbuf[:, :])
                                p.cinc(st_, "ev", 1)

            @blk.gpsimd
            def _(g):
                for tb in range(ntb):
                    t0 = tb * 512
                    p.wge(g, "ev", bE + 17 * (tb + 1))
                    for j in range(NCH):
                        p.dma(g, x_res[j * 128:(j + 1) * 128, t0:t0 + 512],
                              orow[:, j * 512:(j + 1) * 512], "dP2",
                              accum_op=ALU.add)
                p.wait_all(g, "dP2")



L1_KS = [3, 3, 7, 7, 11, 11, 21, 21]
L1_PARTS = {"route1", "xnt", "conv", "oproj", "route2", "cast", "sparse"}
L1_NE = [8]


def emit_l1_attn(p):
    nc = p.nc
    xn = [p.sb[f"xn{c}"] for c in range(NCH)]
    o_scr = p.dram["o_scr"].ap()
    if "route1" in L1_PARTS:
        emit_l1_route1(p)
    if "xnt" in L1_PARTS:
        emit_l1_xnt(p)
    cast_jobs = []
    if "cast" in L1_PARTS:
        w1s, w1d = p.dram["l1_W1f"].ap(), p.dram["l1_w1q"].ap()
        w2s, w2d = p.dram["l1_W2f"].ap(), p.dram["l1_w2q"].ap()
        for r in range(0, 8 * DM, 512):
            cast_jobs.append((w1d[r:r + 512, :], w1s[r:r + 512, :]))
        for r in range(0, 8 * 4096, 2048):
            cast_jobs.append((w2d[r:r + 2048, :], w2s[r:r + 2048, :]))
    if "conv" not in L1_PARTS:
        return

    from contextlib import ExitStack
    with ExitStack() as stck:
        en = lambda *a: stck.enter_context(nc.sbuf_tensor(*a))
        win_sb = en("l1_win", [128, NCH * 1024], BF16)
        binc = en("l1_binc", [128, NCH], F32)
        xp = [en(f"l1_xp{c}", [128, T], BF16) for c in range(NCH)]
        wc_sb = en("l1_wc", [128, 4], BF16)
        c4 = en("l1_c4", [4, T], BF16)
        abc = en("l1_abc", [128, 672], F32)
        vcol = en("l1_vcol", [128, 32], F32)
        bcol = en("l1_bcol", [128, 8 * 21], F32)
        alc = en("l1_alc", [128, 16], F32)
        arow = en("l1_arow", [1, 672], F32)
        alr = en("l1_alr", [1, 8], F32)
        xpad = en("l1_xpad", [128, SEQ + 32], BF16)
        ybuf = en("l1_ybuf", [128, SEQ], BF16)
        ydyn = en("l1_ydyn", [128, SEQ], F32)
        ybase = en("l1_ybase", [128, SEQ], F32)
        cb = en("l1_cb", [128, SEQ], BF16)
        oh_sb = en("l1_oh", [128, SEQ], BF16)

        with nc.Block() as blk:
            bP_ = p.base("dP")
            bMM = p.base("mm")
            bEV = p.base("ev")
            bAQ = p.base("aq")

            @blk.gpsimd
            def _(g):
                w = p.dram["l1_WinT"].ap()
                for c in range(NCH):
                    p.dma(g, win_sb[:, c * 1024:(c + 1) * 1024],
                          w[c * 128:(c + 1) * 128, :], "dP")
                p.dma(g, arow[:, :],
                      p.dram["L1_attn_A"].ap().rearrange(
                          "h r k -> (h r k)").rearrange(
                          "(o a) -> o a", o=1), "dP")
                p.dma(g, vcol[:, :], p.dram["l1_V_T"].ap(), "dP")
                p.dma(g, bcol[:, :], p.dram["l1_base_T"].ap(), "dP")
                p.dma(g, alr[:, :],
                      p.dram["L1_attn_alphas"].ap().rearrange(
                          "(o a) -> o a", o=1), "dP")
                g.memset(xpad[:, :], 0.0)

            @blk.scalar
            def _(s):
                p.wge(s, "dP", bP_ + 16 * (NCH + 4))
                a = s.activation(alr[0:1, :], alr[0:1, :], AF.Sigmoid)
                p.cinc(a, "aq", 1)

            @blk.tensor
            def _(t):
                p.wge(t, "aq", bAQ + 1)
                for jj in range(2):
                    w_ = 512 if jj == 0 else 160
                    mm = t.matmul(p.ps[0][:, 0:w_],
                                  p.sb["ones_row_f32"][:, :],
                                  arow[0:1, jj * 512:jj * 512 + w_],
                                  start=True, stop=True)
                    p.cinc(mm, "mm", 1)
                    p.wge(t, "ev", bEV + jj + 1)
                mm = t.matmul(p.ps[1][:, 0:8], p.sb["ones_row_f32"][:, :],
                              alr[0:1, :], start=True, stop=True)
                p.cinc(mm, "mm", 1)

            @blk.vector
            def _(v):
                for jj in range(2):
                    w_ = 512 if jj == 0 else 160
                    p.wge(v, "mm", bMM + jj + 1)
                    cp = v.tensor_copy(abc[:, jj * 512:jj * 512 + w_],
                                       p.ps[0][:, 0:w_])
                    p.cinc(cp, "ev", 1)
                p.wge(v, "mm", bMM + 3)
                cp = v.tensor_copy(alc[:, 0:8], p.ps[1][:, 0:8])
                p.cinc(cp, "x1", 1)
                p.wge(v, "x1", p.val("x1"))
                cp = v.tensor_scalar(alc[:, 8:16], alc[:, 0:8], -1.0, 1.0,
                                     ALU.mult, ALU.add)
                p.cinc(cp, "x1", 1)

        emit_pervec(p, "L1_attn_b_in", DM, binc)

        with nc.Block() as blk:
            bMM = p.base("mm")
            bEV = p.base("ev")

            @blk.tensor
            def _(t):
                k = 0
                for ch in range(NCH):
                    for js in range(T // 512):
                        ps = p.ps[k % 2]
                        if k >= 2:
                            p.wge(t, "ev", bEV + k - 1)
                        for c2 in range(NCH):
                            mm = t.matmul(
                                ps[:, :],
                                win_sb[:, c2 * 1024 + ch * 128:
                                       c2 * 1024 + (ch + 1) * 128],
                                xn[c2][:, js * 512:(js + 1) * 512],
                                start=(c2 == 0), stop=(c2 == NCH - 1))
                        p.cinc(mm, "mm", 1)
                        k += 1

            @blk.vector
            def _(v):
                k = 0
                for ch in range(NCH):
                    for js in range(T // 512):
                        ps = p.ps[k % 2]
                        p.wge(v, "mm", bMM + k + 1)
                        cp = v.tensor_scalar_add(
                            xp[ch][:, js * 512:(js + 1) * 512], ps[:, :],
                            binc[:, ch:ch + 1])
                        p.cinc(cp, "ev", 1)
                        k += 1

        for h in range(8):
            K_h = L1_KS[h]
            pad = K_h // 2
            with nc.Block() as blk:
                bMM = p.base("mm")
                bEV = p.base("ev")
                bP_ = p.base("dP")

                @blk.gpsimd
                def _(g):
                    wc = p.dram["L1_attn_Wc"].ap()
                    p.dma(g, wc_sb[:, :], wc[h], "dP")

                @blk.tensor
                def _(t):
                    p.wge(t, "dP", bP_ + 16)
                    for js in range(T // 512):
                        ps = p.ps[js % 2]
                        if js >= 2:
                            p.wge(t, "ev", bEV + js - 1)
                        mm = t.matmul(ps[0:4, :], wc_sb[:, :],
                                      xp[h][:, js * 512:(js + 1) * 512],
                                      start=True, stop=True)
                        p.cinc(mm, "mm", 1)

                @blk.vector
                def _(v):
                    for js in range(T // 512):
                        ps = p.ps[js % 2]
                        p.wge(v, "mm", bMM + js + 1)
                        cp = v.tensor_copy(c4[0:4, js * 512:(js + 1) * 512],
                                           ps[0:4, :])
                        p.cinc(cp, "ev", 1)

            for s in range(NSEQ):
                with nc.Block() as blk:
                    bMM = p.base("mm")
                    bEV = p.base("ev")
                    bAQ = p.base("aq")
                    bP_ = p.base("dP")
                    t0 = s * SEQ

                    @blk.tensor
                    def _(t):
                        for r in range(4):
                            if r >= 1:
                                p.wge(t, "ev", bEV + r)
                            for js in range(4):
                                if js >= 2:
                                    p.wge(t, "aq", bAQ + 4 * r + js - 1)
                                mm = t.matmul(
                                    p.ps[js % 2][:, :],
                                    p.sb["eye4r"][0:4, r * 128:(r + 1) * 128],
                                    c4[0:4, t0 + js * 512:t0 + (js + 1) * 512],
                                    start=True, stop=True)
                                p.cinc(mm, "mm", 1)

                    @blk.scalar
                    def _(sc):
                        for r in range(4):
                            for js in range(4):
                                p.wge(sc, "mm", bMM + 4 * r + js + 1)
                                a = sc.activation(
                                    cb[:, js * 512:(js + 1) * 512],
                                    p.ps[js % 2][:, :], AF.Copy)
                                p.cinc(a, "aq", 1)

                    @blk.vector
                    def _(v):
                        cp = v.tensor_copy(xpad[:, 16:16 + SEQ],
                                           xp[h][:, t0:t0 + SEQ])
                        p.cinc(cp, "x1", 1)
                        p.wge(v, "x1", p.val("x1"))
                        for kk in range(K_h):
                            off = 16 + kk - pad
                            if kk == 0:
                                st_ = v.tensor_scalar(
                                    ybase[:, :], xpad[:, off:off + SEQ],
                                    bcol[:, h * 21 + kk:h * 21 + kk + 1],
                                    None, ALU.mult)
                            else:
                                st_ = v.scalar_tensor_tensor(
                                    ybase[:, :], xpad[:, off:off + SEQ],
                                    bcol[:, h * 21 + kk:h * 21 + kk + 1],
                                    ybase[:, :], ALU.mult, ALU.add)
                            p.cinc(st_, "x1", 1)
                            p.wge(v, "x1", p.val("x1"))
                        for r in range(4):
                            p.wge(v, "aq", bAQ + 4 * (r + 1))
                            for kk in range(K_h):
                                off = 16 + kk - pad
                                ai = h * 84 + r * 21 + kk
                                if kk == 0:
                                    st_ = v.tensor_scalar(
                                        ybuf[:, :], xpad[:, off:off + SEQ],
                                        abc[:, ai:ai + 1], None, ALU.mult)
                                else:
                                    st_ = v.scalar_tensor_tensor(
                                        ybuf[:, :], xpad[:, off:off + SEQ],
                                        abc[:, ai:ai + 1], ybuf[:, :],
                                        ALU.mult, ALU.add)
                                p.cinc(st_, "x1", 1)
                                p.wge(v, "x1", p.val("x1"))
                            vi = 4 * h + r
                            if r == 0:
                                st_ = v.scalar_tensor_tensor(
                                    ydyn[:, :], ybuf[:, :],
                                    vcol[:, vi:vi + 1], cb[:, :],
                                    ALU.mult, ALU.mult)
                                p.cinc(st_, "x1", 1)
                                p.wge(v, "x1", p.val("x1"))
                            else:
                                st_ = v.scalar_tensor_tensor(
                                    ybuf[:, :], ybuf[:, :],
                                    vcol[:, vi:vi + 1], cb[:, :],
                                    ALU.mult, ALU.mult)
                                p.cinc(st_, "x1", 1)
                                p.wge(v, "x1", p.val("x1"))
                                st_ = v.tensor_add(ydyn[:, :], ydyn[:, :],
                                                   ybuf[:, :])
                                p.cinc(st_, "x1", 1)
                                p.wge(v, "x1", p.val("x1"))
                            p.cinc(v.engine_nop(), "ev", 1)
                        st_ = v.tensor_scalar_mul(ydyn[:, :], ydyn[:, :],
                                                  alc[:, h:h + 1])
                        p.cinc(st_, "x1", 1)
                        p.wge(v, "x1", p.val("x1"))
                        st_ = v.scalar_tensor_tensor(
                            oh_sb[:, :], ybase[:, :], alc[:, 8 + h:9 + h],
                            ydyn[:, :], ALU.mult, ALU.add)
                        p.cinc(st_, "ev", 1)

                    @blk.gpsimd
                    def _(g):
                        p.wge(g, "ev", bEV + 5)
                        p.dma(g, o_scr[h * 128:(h + 1) * 128, t0:t0 + SEQ],
                              oh_sb[:, :], "dP")
                        p.wait_all(g, "dP")
                        for _ in range(2):
                            if cast_jobs:
                                dst_, src_ = cast_jobs.pop(0)
                                p.dma(g, dst_, src_, "dP")

    if cast_jobs:
        with nc.Block() as blk:
            @blk.gpsimd
            def _(g):
                while cast_jobs:
                    dst_, src_ = cast_jobs.pop(0)
                    p.dma(g, dst_, src_, "dP")
    if "oproj" in L1_PARTS:
        bco = _sbuf_g(p, "l1_bco", [128, NCH], F32)
        emit_pervec(p, "L1_attn_b_out", DM, bco)
        emit_oproj(p, "l1o", "l1_WoutT", "o_scr", bias_col=bco)


def emit_l1_route1(p):
    nc = p.nc
    xn = [p.sb[f"xn{c}"] for c in range(NCH)]
    sb = p.sb
    NT = T // 128
    BIG = 1e6
    sb["l1_oh1"] = _sbuf_g(p, "l1_oh1", [128, NT * 8], BF16)
    sb["l1_oh2"] = _sbuf_g(p, "l1_oh2", [128, NT * 8], BF16)
    sb["l1_w1c"] = _sbuf_g(p, "l1_w1c", [128, NT], F32)
    sb["l1_w2c"] = _sbuf_g(p, "l1_w2c", [128, NT], F32)
    sb["l1_cum1"] = _sbuf_g(p, "l1_cum1", [128, NT], F32)
    sb["l1_cum2"] = _sbuf_g(p, "l1_cum2", [128, NT], F32)
    sb["l1_carr1"] = _sbuf_g(p, "l1_carr1", [1, NT * 8], F32)
    sb["l1_carr2"] = _sbuf_g(p, "l1_carr2", [1, NT * 8], F32)
    sb["l1_cnt"] = _sbuf_g(p, "l1_cnt", [1, 32], F32)
    sb["l1_kw0"] = _sbuf_g(p, "l1_kw0", [128, NT], F32)
    sb["l1_kw1"] = _sbuf_g(p, "l1_kw1", [128, NT], F32)
    sb["l1_iot8"] = _sbuf_g(p, "l1_iot8", [128, 8], F32)
    sb["l1_cmb"] = _sbuf_g(p, "l1_cmb", [8, T], BF16)
    oh_all = (sb["l1_oh1"], sb["l1_oh2"])
    w_all = (sb["l1_w1c"], sb["l1_w2c"])
    cum_all = (sb["l1_cum1"], sb["l1_cum2"])
    carr_all = (sb["l1_carr1"], sb["l1_carr2"])

    from contextlib import ExitStack
    with ExitStack() as stck:
        en = lambda *a: stck.enter_context(nc.sbuf_tensor(*a))
        wg_sb = en("l1_wg", [128, NCH * 8], BF16)
        eb = en("l1_eb", [128, 8], F32)
        pr = en("l1_pr", [128, 8], F32)
        pr2 = en("l1_pr2", [128, 8], F32)
        eq = en("l1_eq", [128, 8], F32)
        tt8 = en("l1_tt8", [128, 8], F32)
        colA = en("l1_colA", [128, 8], F32)
        carry = en("l1_carry", [1, 16], F32)

        with nc.Block() as blk:
            bP_ = p.base("dP")
            bMM = p.base("mm")
            bEV = p.base("ev")
            bAQ = p.base("aq")

            @blk.gpsimd
            def _(g):
                wg = p.dram["L1_ffn_Wg"].ap()
                for c in range(NCH):
                    p.dma(g, wg_sb[:, c * 8:(c + 1) * 8],
                          wg[c * 128:(c + 1) * 128, :], "dP")
                g.iota(sb["l1_iot8"][:, :], [[1, 8]], base=0,
                       channel_multiplier=0,
                       allow_small_or_imprecise_dtypes=True)
                g.memset(carry[:, :], 0.0)

            @blk.tensor
            def _(t):
                p.wge(t, "dP", bP_ + 16 * NCH)
                for i in range(NT):
                    ps = p.ps[i % 2]
                    if i >= 2:
                        p.wge(t, "ev", bEV + i - 1)
                    for c in range(NCH):
                        mm = t.matmul(ps[:, 0:8],
                                      xn[c][:, i * 128:(i + 1) * 128],
                                      wg_sb[:, c * 8:(c + 1) * 8],
                                      start=(c == 0), stop=(c == NCH - 1))
                    p.cinc(mm, "mm", 1)

            @blk.scalar
            def _(s):
                for i in range(NT):
                    p.wge(s, "x1", p.base("x1") + 10 * i + 1)
                    a = s.activation(eb[:, :], p.ps[i % 2][0:128, 0:8],
                                     AF.Exp, bias=colA[:, 0:1],
                                     accum_out=colA[:, 1:2])
                    p.cinc(a, "aq", 1)

            @blk.vector
            def _(v):
                bX = p.base("x1")
                for i in range(NT):
                    ps = p.ps[i % 2]
                    p.wge(v, "mm", bMM + i + 1)
                    cp = v.reduce_max(colA[:, 0:1], ps[0:128, 0:8], AX.X,
                                      negate=True)
                    p.cinc(cp, "x1", 1)
                    p.wge(v, "x1", p.val("x1"))
                    v.engine_nop().then_inc(p.sems.inc("x1", 9), 9)
                    p.wge(v, "aq", bAQ + i + 1)
                    chain = [0]

                    def op(fn):
                        st_ = fn()
                        p.cinc(st_, "x2", 1)
                        p.wge(v, "x2", p.val("x2"))
                    iot = sb["l1_iot8"]
                    op(lambda: v.reciprocal(colA[:, 2:3], colA[:, 1:2]))
                    op(lambda: v.tensor_scalar_mul(pr[:, :], eb[:, :],
                                                   colA[:, 2:3]))
                    op(lambda: v.reduce_max(colA[:, 3:4], pr[:, :], AX.X))
                    op(lambda: v.tensor_single_scalar(eq[:, :], pr[:, :],
                                                      colA[:, 3:4],
                                                      ALU.is_equal))
                    op(lambda: v.scalar_tensor_tensor(tt8[:, :], eq[:, :],
                                                      -BIG, iot[:, :],
                                                      ALU.mult, ALU.add))
                    op(lambda: v.tensor_reduce(colA[:, 4:5], tt8[:, :],
                                               AX.X, ALU.min))
                    op(lambda: v.tensor_scalar_add(colA[:, 4:5],
                                                   colA[:, 4:5], BIG))
                    op(lambda: v.tensor_single_scalar(
                        oh_all[0][:, 8 * i:8 * i + 8], iot[:, :],
                        colA[:, 4:5], ALU.is_equal))
                    op(lambda: v.tensor_scalar(tt8[:, :],
                                               oh_all[0][:, 8 * i:8 * i + 8],
                                               -1.0, 1.0, ALU.mult, ALU.add))
                    op(lambda: v.tensor_mul(pr2[:, :], tt8[:, :], pr[:, :]))
                    op(lambda: v.reduce_max(colA[:, 5:6], pr2[:, :], AX.X))
                    op(lambda: v.tensor_single_scalar(eq[:, :], pr2[:, :],
                                                      colA[:, 5:6],
                                                      ALU.is_equal))
                    op(lambda: v.scalar_tensor_tensor(tt8[:, :], eq[:, :],
                                                      -BIG, iot[:, :],
                                                      ALU.mult, ALU.add))
                    op(lambda: v.tensor_reduce(colA[:, 6:7], tt8[:, :],
                                               AX.X, ALU.min))
                    op(lambda: v.tensor_scalar_add(colA[:, 6:7],
                                                   colA[:, 6:7], BIG))
                    op(lambda: v.tensor_single_scalar(
                        oh_all[1][:, 8 * i:8 * i + 8], iot[:, :],
                        colA[:, 6:7], ALU.is_equal))
                    op(lambda: v.tensor_add(colA[:, 7:8], colA[:, 3:4],
                                            colA[:, 5:6]))
                    op(lambda: v.tensor_scalar_add(colA[:, 7:8],
                                                   colA[:, 7:8], 1e-9))
                    op(lambda: v.reciprocal(colA[:, 7:8], colA[:, 7:8]))
                    op(lambda: v.tensor_mul(w_all[0][:, i:i + 1],
                                            colA[:, 3:4], colA[:, 7:8]))
                    op(lambda: v.tensor_mul(w_all[1][:, i:i + 1],
                                            colA[:, 5:6], colA[:, 7:8]))
                    p.cinc(v.engine_nop(), "ev", 1)

        with nc.Block() as blk:
            bMM = p.base("mm")
            bM2 = p.base("mm2")
            bEV = p.base("ev")

            @blk.tensor
            def _(t):
                for k in range(2):
                    for i in range(NT):
                        u = k * NT + i
                        if u >= 1:
                            p.wge(t, "ev", bEV + 2 * u)
                        mm = t.matmul(p.ps[0][:, 0:8],
                                      p.sb["triu_bf16"][:, :],
                                      oh_all[k][:, 8 * i:8 * i + 8],
                                      start=True, stop=True)
                        p.cinc(mm, "mm", 1)
                        mm = t.matmul(p.ps[1][0:1, 0:8],
                                      p.sb["ones_col_bf16"][:, :],
                                      oh_all[k][:, 8 * i:8 * i + 8],
                                      start=True, stop=True)
                        p.cinc(mm, "mm2", 1)

            @blk.vector
            def _(v):
                for k in range(2):
                    for i in range(NT):
                        u = k * NT + i

                        def op(st_):
                            p.cinc(st_, "x2", 1)
                            p.wge(v, "x2", p.val("x2"))
                        op(v.tensor_copy(carr_all[k][0:1, 8 * i:8 * i + 8],
                                         carry[0:1, 8 * k:8 * k + 8]))
                        p.wge(v, "mm", bMM + u + 1)
                        op(v.tensor_mul(tt8[:, :], p.ps[0][0:128, 0:8],
                                        oh_all[k][:, 8 * i:8 * i + 8]))
                        op(v.reduce_sum(cum_all[k][:, i:i + 1], tt8[:, :],
                                        AX.X))
                        p.wge(v, "mm2", bM2 + u + 1)
                        op(v.tensor_add(carry[0:1, 8 * k:8 * k + 8],
                                        carry[0:1, 8 * k:8 * k + 8],
                                        p.ps[1][0:1, 0:8]))
                        if i == NT // 2 - 1:
                            op(v.tensor_copy(sb["l1_cnt"][0:1,
                                                          8 * k:8 * k + 8],
                                             carry[0:1, 8 * k:8 * k + 8]))
                        p.cinc(v.engine_nop(), "ev", 2)
                    op(v.tensor_sub(sb["l1_cnt"][0:1, 16 + 8 * k:24 + 8 * k],
                                    carry[0:1, 8 * k:8 * k + 8],
                                    sb["l1_cnt"][0:1, 8 * k:8 * k + 8]))
                p.cinc(v.engine_nop(), "ev", 1)

        with nc.Block() as blk:
            bEV = p.base("ev")

            @blk.gpsimd
            def _(g):
                cc_in = p.dram["l1_cc_in"]
                cc_out = p.dram["l1_cc_out"]
                p.wait_all(g, "ev")
                p.dma(g, cc_in.ap().rearrange("a b -> (a b)").rearrange(
                    "(o a) -> o a", o=1), sb["l1_cnt"][0:1, :], "dP")
                p.wait_all(g, "dP")
                cc = g.collective_compute(
                    "AllGather", ALU.bypass,
                    replica_groups=[list(range(NCORE))],
                    ins=[cc_in.ap().opt()], outs=[cc_out.ap().opt()])
                p.cinc(cc, "cc", 1)


def emit_l1_route2(p):
    nc = p.nc
    sb = p.sb
    NT = T // 128
    oh_all = (sb["l1_oh1"], sb["l1_oh2"])
    w_all = (sb["l1_w1c"], sb["l1_w2c"])
    cum_all = (sb["l1_cum1"], sb["l1_cum2"])
    carr_all = (sb["l1_carr1"], sb["l1_carr2"])
    from contextlib import ExitStack
    with ExitStack() as stck:
        en = lambda *a: stck.enter_context(nc.sbuf_tensor(*a))
        ag = en("l1_ag", [32, 8], F32)
        s16 = en("l1_s16", [32, 5], F32)
        offs_e = en("l1_offse", [8, 4], F32)
        offr = en("l1_offr", [1, 32], F32)
        orow1 = en("l1_orow1", [1, 8], F32)
        t8 = en("l1_t8", [128, 8], F32)
        tcol = en("l1_tcol", [128, 4], F32)
        cmt = en("l1_cmt", [128, 8], BF16)
        with nc.Block() as blk:
            bP_ = p.base("dP")
            bMM = p.base("mm")
            bEV = p.base("ev")

            @blk.gpsimd
            def _(g):
                p.wge(g, "cc", p.val("cc"))
                p.dma(g, ag[:, :], p.dram["l1_cc_out"].ap(), "dP")
                p.dma(g, s16[:, :], p.dram["l1_S32"].ap(), "dP")

            @blk.tensor
            def _(t):
                p.wge(t, "dP", bP_ + 32)
                mm = t.matmul(p.ps[0][0:8, 0:5], ag[:, :], s16[:, :],
                              start=True, stop=True)
                p.cinc(mm, "mm", 1)
                p.wge(t, "ev", bEV + 2)
                for k in range(4):
                    mm = t.matmul(p.ps[1][0:1, 0:8], offs_e[:, k:k + 1],
                                  p.sb["ident_f32"][0:8, 0:8],
                                  start=True, stop=True)
                    p.cinc(mm, "mm", 1)
                    p.wge(t, "ev", bEV + 3 + k)

            @blk.vector
            def _(v):
                def op(st_):
                    p.cinc(st_, "x2", 1)
                    p.wge(v, "x2", p.val("x2"))
                p.wge(v, "mm", bMM + 1)
                op(v.tensor_copy(offs_e[:, 0:1], p.ps[0][0:8, 0:1]))
                op(v.tensor_scalar_min(t8[0:8, 0:1], p.ps[0][0:8, 1:2],
                                       2048.0))
                op(v.tensor_add(offs_e[:, 1:2], t8[0:8, 0:1],
                                p.ps[0][0:8, 2:3]))
                op(v.tensor_copy(offs_e[:, 2:3], p.ps[0][0:8, 3:4]))
                op(v.tensor_add(offs_e[:, 3:4], t8[0:8, 0:1],
                                p.ps[0][0:8, 4:5]))
                p.cinc(v.engine_nop(), "ev", 2)
                for k in range(4):
                    p.wge(v, "mm", bMM + 1 + k + 1)
                    op(v.tensor_copy(offr[0:1, 8 * k:8 * k + 8],
                                     p.ps[1][0:1, 0:8]))
                    p.cinc(v.engine_nop(), "ev", 1)

        with (nc.psum_tensor("l1r_pb0", [8, 128], BF16) as pt0,
              nc.psum_tensor("l1r_pb1", [8, 128], BF16) as pt1,
              nc.Block() as blk):
            bMM = p.base("mm")
            bM2 = p.base("mm2")
            bEV = p.base("ev")
            bE2 = p.base("ev2")

            @blk.tensor
            def _(t):
                for i in range(NT):
                    for k in range(2):
                        u = 2 * i + k
                        if u >= 2:
                            p.wge(t, "ev", bEV + 2 * (u - 1))
                        p.wge(t, "ev", bEV + 2 * u + 1)
                        mm = t.matmul(p.ps[k][:, 0:8],
                                      p.sb["ones_row_f32"][:, :],
                                      orow1[0:1, :], start=True, stop=True)
                        p.cinc(mm, "mm", 1)
                    p.wge(t, "ev", bEV + 2 * (2 * i + 1) + 2)
                    pt = (pt0, pt1)[i % 2]
                    if i >= 2:
                        p.wge(t, "ev2", bE2 + i - 1)
                    mm = t.transpose(pt[:, :], cmt[:, :],
                                     p.sb["ident_bf16"][:, :])
                    p.cinc(mm, "mm2", 1)

            @blk.vector
            def _(v):
                def op(st_):
                    p.cinc(st_, "x2", 1)
                    p.wge(v, "x2", p.val("x2"))
                for i in range(NT):
                    half = i // (NT // 2)
                    for k in range(2):
                        u = 2 * i + k
                        j = k + 2 * half
                        op(v.tensor_add(orow1[0:1, :],
                                        offr[0:1, 8 * j:8 * j + 8],
                                        carr_all[k][0:1, 8 * i:8 * i + 8]))
                        p.cinc(v.engine_nop(), "ev", 1)
                        p.wge(v, "mm", bMM + u + 1)
                        op(v.tensor_mul(t8[:, :], p.ps[k][0:128, 0:8],
                                        oh_all[k][:, 8 * i:8 * i + 8]))
                        op(v.reduce_sum(tcol[:, 0:1], t8[:, :], AX.X))
                        op(v.tensor_add(tcol[:, 1:2], tcol[:, 0:1],
                                        cum_all[k][:, i:i + 1]))
                        op(v.tensor_scalar_add(tcol[:, 1:2], tcol[:, 1:2],
                                               -1.0))
                        op(v.tensor_single_scalar(tcol[:, 2:3], tcol[:, 1:2],
                                                  2048.0, ALU.is_lt))
                        op(v.tensor_mul(tcol[:, 3:4], tcol[:, 2:3],
                                        w_all[k][:, i:i + 1]))
                        kwx = sb[f"l1_kw{k}"]
                        op(v.tensor_copy(kwx[:, i:i + 1], tcol[:, 3:4]))
                        if k == 0:
                            op(v.tensor_scalar(
                                cmt[:, :], oh_all[0][:, 8 * i:8 * i + 8],
                                tcol[:, 3:4], None, ALU.mult))
                        else:
                            op(v.scalar_tensor_tensor(
                                cmt[:, :], oh_all[1][:, 8 * i:8 * i + 8],
                                tcol[:, 3:4], cmt[:, :], ALU.mult, ALU.add))
                        p.cinc(v.engine_nop(), "ev", 1)
                    pt = (pt0, pt1)[i % 2]
                    p.wge(v, "mm2", bM2 + i + 1)
                    cp = v.tensor_copy(sb["l1_cmb"][0:8,
                                                    i * 128:(i + 1) * 128],
                                       pt[:, :])
                    p.cinc(cp, "ev2", 1)


def emit_l1_xnt(p):
    nc = p.nc
    xn = [p.sb[f"xn{c}"] for c in range(NCH)]
    xn_t = p.dram["xn_t"].ap()
    y_moe = p.dram["y_moe"].ap()
    ntile = T // 128
    with (
        nc.sbuf_tensor("xnt_st0", [128, DM], BF16) as st0,
        nc.sbuf_tensor("xnt_st1", [128, DM], BF16) as st1,
        nc.sbuf_tensor("xnt_st2", [128, DM], BF16) as st2,
        nc.sbuf_tensor("xnt_st3", [128, DM], BF16) as st3,
        nc.sbuf_tensor("xnt_z", [128, DM], F32) as zt,
        nc.psum_tensor("xnt_p0", [128, 128], BF16) as p0,
        nc.psum_tensor("xnt_p1", [128, 128], BF16) as p1,
        nc.Block() as blk,
    ):
        sts = (st0, st1, st2, st3)
        identb = p.sb["ident_bf16"]
        bM = p.base("mm5")
        bE = p.base("ev4")
        bO = p.base("dX")
        dP_all = p.val("dP")

        @blk.tensor
        def _(t):
            p.wge(t, "dP", dP_all)
            for it in range(ntile):
                for c in range(NCH):
                    pp = (p0, p1)[c % 2]
                    k = it * NCH + c
                    if k >= 2:
                        p.wge(t, "ev4", bE + k - 1)
                    mm = t.transpose(pp[:, :],
                                     xn[c][:, it * 128:(it + 1) * 128],
                                     identb[:, :])
                    p.cinc(mm, "mm5", 1)

        @blk.vector
        def _(v):
            for it in range(ntile):
                st = sts[it % 4]
                if it >= 4:
                    p.wge(v, "dX", bO + 16 * (it - 3))
                for c in range(NCH):
                    pp = (p0, p1)[c % 2]
                    k = it * NCH + c
                    p.wge(v, "mm5", bM + k + 1)
                    cp = v.tensor_copy(st[:, c * 128:(c + 1) * 128],
                                       pp[:, :])
                    p.cinc(cp, "ev4", 1)

        @blk.gpsimd
        def _(g):
            g.memset(zt[:, :], 0.0)
            for r in range(2 * T // 128):
                p.dma(g, y_moe[r * 128:(r + 1) * 128, :], zt[:, :], "dZ")
            for it in range(ntile):
                st = sts[it % 4]
                p.wge(g, "ev4", bE + (it + 1) * NCH)
                p.dma(g, xn_t[it * 128:(it + 1) * 128, :], st[:, :], "dX")
            p.wait_all(g, "dX")
            p.wait_all(g, "dZ")


def emit_l1_sidx(p):
    nc = p.nc
    sb = p.sb
    NT = T // 128
    I32 = mybir.dt.int32
    oh_all = (sb["l1_oh1"], sb["l1_oh2"])
    kw = (sb["l1_kw0"], sb["l1_kw1"])
    din = p.dram["l1_Din"].ap()
    dout = p.dram["l1_Dout"].ap()
    from contextlib import ExitStack
    with ExitStack() as stck:
        en = lambda *a: stck.enter_context(nc.sbuf_tensor(*a))
        km8 = en("sx_km", [128, 8], BF16)
        t8r = en("sx_r", [128, 8], F32)
        t8p = en("sx_p", [128, 8], F32)
        f8 = en("sx_f", [128, 8], F32)
        tt8 = en("sx_t", [128, 8], F32)
        baseS = en("sx_base", [128, 8], F32)
        scarry = en("sx_carry", [1, 8], F32)
        col = en("sx_col", [128, 8], F32)
        fl = en("sx_fl", [128, 2], F32)
        strash = en("sx_strash", [128, 1], F32)
        tokf = en("sx_tok", [128, 1], F32)
        vt = en("sx_vt", [128, 2], F32)
        oi = en("sx_oi", [128, 4], I32)
        vi = en("sx_vi", [128, 4], I32)
        zfill = en("sx_zf", [128, S_TOT // 128], I32)
        sfill = en("sx_sf", [128, S_TOT // 128], I32)
        with (
            nc.psum_tensor("sx_pA", [128, 8], F32) as pA,
            nc.psum_tensor("sx_pB", [1, 8], F32) as pB,
            nc.psum_tensor("sx_pC", [128, 8], F32) as pC,
            nc.Block() as blk,
        ):
            bG = p.base("dG")
            bM = p.base("mm6")
            bV = p.base("ev5")

            @blk.gpsimd
            def _(g):
                g.iota(baseS[:, :], [[S_E, 8]], base=0, channel_multiplier=0,
                       allow_small_or_imprecise_dtypes=True)
                g.iota(tokf[:, :], [[0, 1]], base=0, channel_multiplier=1,
                       allow_small_or_imprecise_dtypes=True)
                g.iota(strash[:, :], [[0, 1]], base=S_TOT,
                       channel_multiplier=1,
                       allow_small_or_imprecise_dtypes=True)
                g.memset(scarry[:, :], 0.0)
                g.memset(zfill[:, :], 0.0)
                g.iota(sfill[:, :], [[0, S_TOT // 128]], base=2 * T,
                       channel_multiplier=1,
                       allow_small_or_imprecise_dtypes=True)
                p.dma(g, din[0:S_TOT, :].rearrange("(a b) o -> a (b o)",
                                                   a=128),
                      zfill[:, :], "dG")
                p.dma(g, dout[0:S_TOT, :].rearrange(
                          "(e p c) o -> p e (c o)", e=8, p=128),
                      sfill[:, :].rearrange("p (e c) -> p e c", e=8), "dG")
                p.wge(g, "dG", bG + 32)
                for i in range(NT):
                    par = i % 2
                    p.wge(g, "ev5", bV + 9 * i + 9)
                    for k in range(2):
                        ins = g.indirect_dma_start(
                            din, IndirectOffsetOnAxis(
                                oi[:, 2 * par + k:2 * par + k + 1], 0),
                            vi[:, 2 * par:2 * par + 1], None)
                        p.cinc(ins, "dG", 16)
                        ins = g.indirect_dma_start(
                            dout, IndirectOffsetOnAxis(
                                oi[:, 2 * par + k:2 * par + k + 1], 0),
                            vi[:, 2 * par + k:2 * par + k + 1], None)
                        p.cinc(ins, "dG", 16)
                p.wait_all(g, "dG")

            @blk.tensor
            def _(t):
                for i in range(NT):
                    p.wge(t, "ev5", bV + 9 * i + 1)
                    t.matmul(pA[:, :], p.sb["triu_bf16"][:, :], km8[:, :],
                             start=True, stop=True)
                    t.matmul(pB[0:1, :], p.sb["ones_col_bf16"][:, :],
                             km8[:, :], start=True, stop=True)
                    mm = t.matmul(pC[:, :], p.sb["ones_row_f32"][:, :],
                                  scarry[0:1, :], start=True, stop=True)
                    p.cinc(mm, "mm6", 1)

            @blk.vector
            def _(v):
                def op(st_):
                    p.cinc(st_, "x2", 1)
                    p.wge(v, "x2", p.val("x2"))
                for i in range(NT):
                    par = i % 2
                    half = i // (NT // 2)
                    if i >= 1:
                        p.wge(v, "mm6", bM + i)
                    op(v.tensor_single_scalar(fl[:, 0:1], kw[0][:, i:i + 1],
                                              0.0, ALU.is_gt))
                    op(v.tensor_single_scalar(fl[:, 1:2], kw[1][:, i:i + 1],
                                              0.0, ALU.is_gt))
                    op(v.tensor_scalar(km8[:, :],
                                       oh_all[0][:, 8 * i:8 * i + 8],
                                       fl[:, 0:1], None, ALU.mult))
                    op(v.scalar_tensor_tensor(km8[:, :],
                                              oh_all[1][:, 8 * i:8 * i + 8],
                                              fl[:, 1:2], km8[:, :],
                                              ALU.mult, ALU.add))
                    p.cinc(v.engine_nop(), "ev5", 1)
                    p.wge(v, "mm6", bM + i + 1)
                    if i >= 2:
                        p.wge(v, "dG", bG + 32 + 64 * (i - 1))
                    op(v.tensor_copy(t8r[:, :], pA[0:128, 0:8]))
                    op(v.tensor_add(t8r[:, :], t8r[:, :], pC[0:128, 0:8]))
                    op(v.tensor_sub(t8r[:, :], t8r[:, :], km8[:, :]))
                    op(v.tensor_single_scalar(f8[:, :], t8r[:, :],
                                              float(S_E), ALU.is_lt))
                    op(v.tensor_add(t8p[:, :], t8r[:, :], baseS[:, :]))
                    op(v.tensor_add(scarry[0:1, :], scarry[0:1, :],
                                    pB[0:1, 0:8]))
                    op(v.tensor_scalar_add(vt[:, 0:1], tokf[:, :],
                                           float(128 * i)))
                    op(v.tensor_scalar_add(vt[:, 1:2], vt[:, 0:1], float(T)))
                    op(v.tensor_copy(vi[:, 2 * par:2 * par + 1], vt[:, 0:1]))
                    op(v.tensor_copy(vi[:, 2 * par + 1:2 * par + 2],
                                     vt[:, 1:2]))
                    for k in range(2):
                        ohk = oh_all[k][:, 8 * i:8 * i + 8]
                        op(v.tensor_mul(tt8[:, :], ohk, t8p[:, :]))
                        op(v.reduce_sum(col[:, 0:1], tt8[:, :], AX.X))
                        op(v.tensor_mul(tt8[:, :], ohk, f8[:, :]))
                        op(v.reduce_sum(col[:, 1:2], tt8[:, :], AX.X))
                        op(v.tensor_mul(col[:, 2:3], col[:, 1:2],
                                        fl[:, k:k + 1]))
                        op(v.tensor_mul(col[:, 3:4], col[:, 0:1],
                                        col[:, 2:3]))
                        op(v.tensor_mul(col[:, 5:6], strash[:, :],
                                        col[:, 2:3]))
                        op(v.tensor_sub(col[:, 6:7], strash[:, :],
                                        col[:, 5:6]))
                        op(v.tensor_add(col[:, 4:5], col[:, 3:4],
                                        col[:, 6:7]))
                        op(v.tensor_copy(oi[:, 2 * par + k:2 * par + k + 1],
                                         col[:, 4:5]))
                    p.cinc(v.engine_nop(), "ev5", 8)


def emit_l1_sparse(p):
    nc = p.nc
    w1 = p.dram["l1_w1q"].ap()
    w2 = p.dram["l1_w2q"].ap()
    din = p.dram["l1_Din"].ap()
    dout = p.dram["l1_Dout"].ap()
    xn_t = p.dram["xn_t"].ap()
    y_moe = p.dram["y_moe"].ap()
    I32 = mybir.dt.int32
    b1c = _sbuf_g(p, "l1_b1c", [128, 8 * 32], F32)
    b2c = _sbuf_g(p, "l1_b2c", [128, 8 * 8], F32)
    for e in range(8):
        emit_pervec(p, f"l1b1_{e}", 4096, b1c[:, e * 32:(e + 1) * 32],
                    src_ap=p.dram["L1_ffn_b1"].ap()[e])
        emit_pervec(p, f"l1b2_{e}", DM, b2c[:, e * 8:(e + 1) * 8],
                    src_ap=p.dram["L1_ffn_b2"].ap()[e])

    ntb = S_E // 512
    nu = 8 * ntb
    NG = S_E // 128
    from contextlib import ExitStack
    with ExitStack() as stck:
        en = lambda *a: stck.enter_context(nc.sbuf_tensor(*a))
        wb = [en("l1f_w0", [128, 8 * 2048], BF16),
              en("l1f_w1", [128, 8 * 2048], BF16)]
        hb = en("l1f_h", [128, 16 * 512], BF16)
        orow = en("l1f_or", [128, NCH * 512], F32)
        xg = en("l1f_xg", [128, NCH * 512], BF16)
        tmb = [en("l1f_tm0", [128, DM], BF16),
               en("l1f_tm1", [128, DM], BF16)]
        ytb = [en("l1f_yt0", [128, 2 * DM], F32),
               en("l1f_yt1", [128, 2 * DM], F32)]
        dinb = en("l1f_din", [128, NG], F32)
        doutb = en("l1f_dout", [128, NG], F32)
        dini = en("l1f_dini", [128, NG], I32)
        douti = en("l1f_douti", [128, NG], I32)
        with (
            nc.psum_tensor("l1f_t0", [128, 128], BF16) as pt0,
            nc.psum_tensor("l1f_t1", [128, 128], BF16) as pt1,
            nc.psum_tensor("l1f_y0", [128, 128], F32) as py0,
            nc.psum_tensor("l1f_y1", [128, 128], F32) as py1,
            nc.Block() as blk,
        ):
            bS = p.base("dS")
            dP_cast = p.val("dP")
            bM = p.base("mm")
            bQ = p.base("aq")
            bE = p.base("ev")
            bD = p.base("dD")
            bG2 = p.base("dG2")
            bY = p.base("dY")
            bM7 = p.base("mm7")
            bE7 = p.base("ev7")
            bM8 = p.base("mm8")
            bE6 = p.base("ev6")
            bX1 = p.base("x1")
            dG_all = p.val("dG")
            dX_all = p.val("dX")
            dZ_all = p.val("dZ")

            @blk.sync
            def _(sy):
                for u in range(nu):
                    e, tb = divmod(u, ntb)
                    if u == 0:
                        p.wge(sy, "dP", dP_cast)
                    if u >= 1:
                        p.wge(sy, "mm", bM + 48 * (u - 1) + 40)
                    for c in range(NCH):
                        p.dma(sy, wb[0][:, c * 2048:(c + 1) * 2048],
                              w1[e * 1024 + c * 128:e * 1024 + (c + 1) * 128,
                                 0:2048], "dS")
                    if u >= 1:
                        p.wge(sy, "mm", bM + 48 * u)
                    for r in range(16):
                        p.dma(sy, wb[1][:, r * 1024:(r + 1) * 1024],
                              w2[e * 4096 + r * 128:
                                 e * 4096 + (r + 1) * 128, :], "dS")
                    p.wge(sy, "mm", bM + 48 * u + 16)
                    for c in range(NCH):
                        p.dma(sy, wb[0][:, c * 2048:(c + 1) * 2048],
                              w1[e * 1024 + c * 128:e * 1024 + (c + 1) * 128,
                                 2048:4096], "dS")
                    p.wge(sy, "mm", bM + 48 * u + 24)
                    for r in range(16):
                        p.dma(sy, wb[1][:, r * 1024:(r + 1) * 1024],
                              w2[e * 4096 + 2048 + r * 128:
                                 e * 4096 + 2048 + (r + 1) * 128, :], "dS")

            @blk.gpsimd
            def _(g):
                def scat(up):
                    ep_, tbp = divmod(up, ntb)
                    for wave in range(2):
                        v = 2 * up + wave
                        yt = ytb[v % 2]
                        p.wge(g, "ev6", bE6 + 16 * (v + 1))
                        for jl2 in range(2):
                            jl = 2 * wave + jl2
                            ins = g.indirect_dma_start(
                                y_moe, IndirectOffsetOnAxis(
                                    douti[:, 4 * tbp + jl:
                                          4 * tbp + jl + 1], 0),
                                yt[:, jl2 * DM:(jl2 + 1) * DM], None)
                            p.cinc(ins, "dY", 16)
                for u in range(nu):
                    e, tb = divmod(u, ntb)
                    if u == 1:
                        p.wge(g, "dZ", dZ_all)
                    if u >= 1:
                        scat(u - 1)
                    if u == 0:
                        p.wge(g, "dG", dG_all)
                        p.wge(g, "dX", dX_all)
                    if tb == 0:
                        p.dma(g, dinb[:, :],
                              din[e * S_E:(e + 1) * S_E, :].rearrange(
                                  "(a b) o -> a (b o)", a=128), "dD")
                        p.dma(g, doutb[:, :],
                              dout[e * S_E:(e + 1) * S_E, :].rearrange(
                                  "(a b) o -> a (b o)", a=128), "dD")
                    p.wge(g, "x1", bX1 + 2 * (e + 1))
                    for jl in range(4):
                        gi = 4 * u + jl
                        if gi >= 2:
                            p.wge(g, "mm7", bM7 + 8 * (gi - 1))
                        ins = g.indirect_dma_start(
                            tmb[gi % 2][:, :], None,
                            xn_t, IndirectOffsetOnAxis(
                                dini[:, 4 * tb + jl:4 * tb + jl + 1], 0))
                        p.cinc(ins, "dG2", 16)
                scat(nu - 1)
                p.wait_all(g, "dY")

            @blk.vector
            def _(v):
                for u in range(nu):
                    e, tb = divmod(u, ntb)
                    if tb == 0:
                        p.wge(v, "dD", bD + 32 * (e + 1))
                        if u >= 1:
                            p.wge(v, "dG2", bG2 + 64 * u)
                            p.wge(v, "dY", bY + 64 * u)
                        cp = v.tensor_copy(dini[:, :], dinb[:, :])
                        p.cinc(cp, "x1", 1)
                        p.wge(v, "x1", p.val("x1"))
                        cp = v.tensor_copy(douti[:, :], doutb[:, :])
                        p.cinc(cp, "x1", 1)
                        p.wge(v, "x1", p.val("x1"))
                    if u >= 1:
                        p.wge(v, "mm", bM + 48 * (u - 1) + 40)
                    for jl in range(4):
                        gi = 4 * u + jl
                        for c in range(NCH):
                            k = 8 * gi + c
                            pp = (pt0, pt1)[k % 2]
                            p.wge(v, "mm7", bM7 + k + 1)
                            cp = v.tensor_copy(
                                xg[:, c * 512 + jl * 128:
                                   c * 512 + (jl + 1) * 128],
                                pp[:, :])
                            p.cinc(cp, "ev7", 1)
                    for m in range(16):
                        hf, j = divmod(m, 8)
                        ps = p.ps[m % 2]
                        base_mm = bM + 48 * u + (16 if hf == 0 else 40)
                        p.wge(v, "mm", base_mm + j + 1)
                        if hf == 0:
                            st_ = v.tensor_scalar(
                                orow[:, j * 512:(j + 1) * 512], ps[:, :],
                                b2c[:, e * 8 + j:e * 8 + j + 1], None,
                                ALU.add)
                            p.cinc(st_, "ev", 1)
                        else:
                            st_ = v.tensor_add(
                                orow[:, j * 512:(j + 1) * 512],
                                orow[:, j * 512:(j + 1) * 512], ps[:, :])
                            p.cinc(st_, "ev", 1)
                    for kk in range(32):
                        wave, r5 = divmod(kk, 16)
                        v5 = 2 * u + wave
                        if r5 == 0 and v5 >= 2:
                            p.wge(v, "dY", bY + 32 * (v5 - 1))
                        sblk, c = divmod(kk, NCH)
                        pp = (py0, py1)[kk % 2]
                        p.wge(v, "mm8", bM8 + 32 * u + kk + 1)
                        cp = v.tensor_copy(
                            ytb[v5 % 2][:, (sblk % 2) * DM + c * 128:
                                        (sblk % 2) * DM + (c + 1) * 128],
                            pp[:, :])
                        p.cinc(cp, "ev6", 1)

            @blk.scalar
            def _(s):
                for u in range(nu):
                    e, tb = divmod(u, ntb)
                    for q in range(32):
                        hf, j = divmod(q, 16)
                        base_mm = bM + 48 * u + (0 if hf == 0 else 24)
                        p.wge(s, "mm", base_mm + j + 1)
                        ps = p.ps[j % 2]
                        a = s.activation(
                            hb[:, j * 512:(j + 1) * 512], ps[:, :],
                            AF.Gelu_apprx_tanh,
                            bias=b1c[:, e * 32 + hf * 16 + j:
                                     e * 32 + hf * 16 + j + 1])
                        p.cinc(a, "aq", 1)

            @blk.tensor
            def _(t):
                for u in range(nu):
                    e, tb = divmod(u, ntb)
                    for jl in range(4):
                        gi = 4 * u + jl
                        p.wge(t, "dG2", bG2 + 16 * (gi + 1))
                        for c in range(NCH):
                            k = 8 * gi + c
                            pp = (pt0, pt1)[k % 2]
                            if k >= 2:
                                p.wge(t, "ev7", bE7 + k - 1)
                            mm = t.transpose(
                                pp[:, :],
                                tmb[gi % 2][:, c * 128:(c + 1) * 128],
                                p.sb["ident_bf16"][:, :])
                            p.cinc(mm, "mm7", 1)
                    p.wge(t, "ev7", bE7 + 32 * (u + 1))
                    for ph in range(4):
                        hf = ph // 2
                        if ph == 0:
                            p.wge(t, "dS", bS + 16 * (48 * u + 8))
                            p.wge(t, "ev", bE + 16 * u)
                        elif ph == 1:
                            p.wge(t, "dS", bS + 16 * (48 * u + 24))
                            p.wge(t, "aq", bQ + 32 * u + 16)
                        elif ph == 2:
                            p.wge(t, "dS", bS + 16 * (48 * u + 32))
                        else:
                            p.wge(t, "dS", bS + 16 * (48 * u + 48))
                            p.wge(t, "aq", bQ + 32 * u + 32)
                        if ph % 2 == 0:
                            for j in range(16):
                                q = hf * 16 + j
                                ps = p.ps[j % 2]
                                if q >= 2:
                                    p.wge(t, "aq", bQ + 32 * u + q - 1)
                                if ph == 2 and j < 2:
                                    p.wge(t, "ev", bE + 16 * u + 7 + j)
                                for c in range(NCH):
                                    mm = t.matmul(
                                        ps[:, :],
                                        wb[0][:, c * 2048 + j * 128:
                                              c * 2048 + (j + 1) * 128],
                                        xg[:, c * 512:(c + 1) * 512],
                                        start=(c == 0), stop=(c == NCH - 1))
                                p.cinc(mm, "mm", 1)
                        else:
                            for j in range(NCH):
                                m = hf * 8 + j
                                ps = p.ps[m % 2]
                                if m >= 2:
                                    p.wge(t, "ev", bE + 16 * u + m - 1)
                                for r in range(16):
                                    mm = t.matmul(
                                        ps[:, :],
                                        wb[1][:, r * 1024 + j * 128:
                                              r * 1024 + (j + 1) * 128],
                                        hb[:, r * 512:(r + 1) * 512],
                                        start=(r == 0), stop=(r == 15))
                                p.cinc(mm, "mm", 1)
                    p.wge(t, "ev", bE + 16 * (u + 1))
                    for kk in range(32):
                        sblk, c = divmod(kk, NCH)
                        pp = (py0, py1)[kk % 2]
                        if kk >= 2:
                            p.wge(t, "ev6", bE6 + 32 * u + kk - 1)
                        mm = t.transpose(
                            pp[:, :],
                            orow[:, c * 512 + sblk * 128:
                                 c * 512 + (sblk + 1) * 128],
                            p.sb["ident_f32"][:, :])
                        p.cinc(mm, "mm8", 1)
                p.wait_all(t, "dY")


def emit_l1_combine(p):
    nc = p.nc
    sb = p.sb
    x_res = p.dram["x_res"].ap()
    y_moe = p.dram["y_moe"].ap()
    NTT = T // 128
    from contextlib import ExitStack
    with ExitStack() as stck:
        en = lambda *a: stck.enter_context(nc.sbuf_tensor(*a))
        y0t = [en("l1c_y0a", [128, DM], F32), en("l1c_y0b", [128, DM], F32)]
        y1t = [en("l1c_y1a", [128, DM], F32), en("l1c_y1b", [128, DM], F32)]
        yc = [en("l1c_ca", [128, DM], F32), en("l1c_cb", [128, DM], F32)]
        est = [en("l1c_ea", [128, DM], F32), en("l1c_eb", [128, DM], F32)]
        with (
            nc.psum_tensor("l1c_p0", [128, 128], F32) as p0,
            nc.psum_tensor("l1c_p1", [128, 128], F32) as p1,
            nc.Block() as blk,
        ):
            bS = p.base("dS")
            bM = p.base("mm9")
            bE = p.base("ev8")
            bA = p.base("dP2")
            dY_all = p.val("dY")

            @blk.sync
            def _(sy):
                p.wge(sy, "dY", dY_all)
                for i in range(NTT):
                    par = i % 2
                    if i >= 2:
                        p.wge(sy, "ev8", bE + 9 * (i - 1))
                    p.dma(sy, y0t[par][:, :],
                          y_moe[i * 128:(i + 1) * 128, :], "dS")
                    p.dma(sy, y1t[par][:, :],
                          y_moe[T + i * 128:T + (i + 1) * 128, :], "dS")

            @blk.vector
            def _(v):
                for i in range(NTT):
                    par = i % 2
                    p.wge(v, "dS", bS + 16 * 2 * (i + 1))
                    if i >= 2:
                        p.wge(v, "mm9", bM + NCH * (i - 1))
                        p.wge(v, "dP2", bA + 16 * NCH * (i - 1))
                    st_ = v.tensor_scalar(yc[par][:, :], y0t[par][:, :],
                                          sb["l1_kw0"][:, i:i + 1], None,
                                          ALU.mult)
                    p.cinc(st_, "x1", 1)
                    p.wge(v, "x1", p.val("x1"))
                    st_ = v.scalar_tensor_tensor(yc[par][:, :], y1t[par][:, :],
                                                 sb["l1_kw1"][:, i:i + 1],
                                                 yc[par][:, :],
                                                 ALU.mult, ALU.add)
                    p.cinc(st_, "ev8", 1)
                    for c in range(NCH):
                        k = i * NCH + c
                        pp = (p0, p1)[c % 2]
                        p.wge(v, "mm9", bM + k + 1)
                        cp = v.tensor_copy(est[par][:, c * 128:(c + 1) * 128],
                                           pp[:, :])
                        p.cinc(cp, "ev8", 1)

            @blk.tensor
            def _(t):
                for i in range(NTT):
                    par = i % 2
                    p.wge(t, "ev8", bE + 9 * i + 1)
                    for c in range(NCH):
                        k = i * NCH + c
                        pp = (p0, p1)[c % 2]
                        if k >= 2:
                            p.wge(t, "ev8", bE + 9 * ((k - 2) // NCH) + 2 +
                                  ((k - 2) % NCH))
                        mm = t.transpose(pp[:, :],
                                         yc[par][:, c * 128:(c + 1) * 128],
                                         p.sb["ident_f32"][:, :])
                        p.cinc(mm, "mm9", 1)

            @blk.gpsimd
            def _(g):
                for i in range(NTT):
                    par = i % 2
                    p.wge(g, "ev8", bE + 9 * (i + 1))
                    for c in range(NCH):
                        p.dma(g, x_res[c * 128:(c + 1) * 128,
                                       i * 128:(i + 1) * 128],
                              est[par][:, c * 128:(c + 1) * 128], "dP2",
                              accum_op=ALU.add)
                p.wait_all(g, "dP2")


def emit_l1_ffn(p):
    nc = p.nc
    if "route2" in L1_PARTS:
        emit_l1_route2(p)
    if "sparse" in L1_PARTS:
        emit_l1_sidx(p)
        emit_l1_sparse(p)
        emit_l1_combine(p)
    _free_l1(p)


EMIT_ATTN = {0: emit_l0_attn, 1: emit_l1_attn, 2: emit_l2_attn, 3: emit_l3_attn}
EMIT_FFN = {0: emit_l0_ffn, 1: emit_l1_ffn, 2: emit_l2_ffn, 3: emit_l3_ffn}


def declare_weights(p, layers):
    nc = p.nc
    d = p.dram

    def din(name, shape, dtype=F32):
        d[name] = nc.dram_tensor(name, shape, dtype, kind="ExternalInput")

    def dint(name, shape, dtype):
        d[name] = nc.dram_tensor(name, shape, dtype, kind="Internal")

    if any(li in layers for li in (0, 1, 2, 3)):
        dint("o_scr", [DM, T], BF16)
    if 2 in layers:
        for nm in ("l2_WqT", "l2_WkT", "l2_WvT", "l2_WoT"):
            din(nm, [DM, DM])
        for nm in ("L2_attn_bq", "L2_attn_bk", "L2_attn_bv", "L2_attn_bo"):
            din(nm, [DM])
        din("l2_w1T", [DM, 4096])
        din("l2_w2T", [DM, 4096])
        din("l2_h1T", [4096, 4096])
        din("l2_h2T", [4096, 4096])
        din("l2_w3T", [4096, DM])
        for nm in ("L2_ffn_w1_s", "L2_ffn_w2_s", "L2_ffn_h1_s",
                   "L2_ffn_h2_s"):
            din(nm, [4096])
        din("L2_ffn_w3_s", [DM])
        dint("l2_w1q", [DM, 4096], BF16)
        dint("l2_w2q", [DM, 4096], BF16)
        dint("l2_h1q", [4096, 4096], BF16)
        dint("l2_h2q", [4096, 4096], BF16)
        dint("l2_w3q", [4096, DM], BF16)
        dint("l2_A", [4096, T], BF16)
        dint("l2_B", [4096, T], BF16)
    if 1 in layers:
        din("l1_WinT", [DM, DM])
        din("l1_WoutT", [DM, DM])
        din("L1_attn_b_in", [DM])
        din("L1_attn_b_out", [DM])
        din("L1_attn_Wc", [8, 128, 4])
        din("L1_attn_A", [8, 4, 21])
        din("l1_V_T", [128, 32])
        din("l1_base_T", [128, 168])
        din("L1_attn_alphas", [8])
        din("L1_ffn_Wg", [DM, 8])
        din("l1_W1f", [8 * DM, 4096])
        din("l1_W2f", [8 * 4096, DM])
        din("L1_ffn_b1", [8, 4096])
        din("L1_ffn_b2", [8, DM])
        din("l1_S32", [32, 5])
        dint("l1_w1q", [8 * DM, 4096], BF16)
        dint("l1_w2q", [8 * 4096, DM], BF16)
        dint("l1_cc_in", [4, 8], F32)
        dint("l1_cc_out", [32, 8], F32)
        dint("xn_t", [T, DM], BF16)
        dint("l1_Din", [S_TOT + 128, 1], mybir.dt.int32)
        dint("l1_Dout", [S_TOT + 128, 1], mybir.dt.int32)
        dint("y_moe", [2 * T + 128, DM], F32)
    if 3 in layers:
        for nm in ("L3_attn_Wq", "L3_attn_Wk", "L3_attn_Wv", "L3_attn_Wg",
                   "L3_attn_Wo"):
            din(nm, [DM, DM])
        din("L3_attn_sk_w1", [DM, 16])
        din("L3_attn_sv_w1", [DM, 16])
        din("L3_attn_sk_w2", [16, 256])
        din("L3_attn_sv_w2", [16, 256])
        din("L3_attn_sk_b2", [256])
        din("L3_attn_sv_b2", [256])
        din("L3_attn_gnorm_w", [256])
        din("l3_upT", [DM, 4096])
        din("l3_downT", [4096, DM])
        dint("l3_upq", [DM, 4096], BF16)
        dint("l3_downq", [4096, DM], BF16)
    if 0 in layers:
        din("l0_wq_cm", [DM, 48])
        din("l0_wk_cm", [DM, 48])
        din("L0_attn_wv", [DM, DM])
        din("L0_attn_wo", [DM, DM])
        din("L0_attn_head_dirs", [16, 3])
        din("L0_attn_scale", [1])
        din("l0_W1T", [DM, 4096])
        din("L0_ffn_b1", [4096])
        din("l0_W2T", [4096, DM])
        din("L0_ffn_b2", [DM])
        dint("l0_wq1_hbm", [DM, 4096], BF16)
        dint("l0_wq2_hbm", [4096, DM], BF16)


def prep_weights_host(p, inputs, layers):
    h = p.host
    f32c = lambda a: np.ascontiguousarray(np.asarray(a), dtype=np.float32)
    if 0 in layers:
        wq = f32c(inputs["L0_attn_wq"])
        wk = f32c(inputs["L0_attn_wk"])
        h["l0_wq_cm"] = np.ascontiguousarray(
            wq.reshape(DM, 16, 3).transpose(0, 2, 1).reshape(DM, 48))
        h["l0_wk_cm"] = np.ascontiguousarray(
            wk.reshape(DM, 16, 3).transpose(0, 2, 1).reshape(DM, 48))
        h["L0_attn_wv"] = f32c(inputs["L0_attn_wv"])
        h["L0_attn_wo"] = f32c(inputs["L0_attn_wo"])
        h["L0_attn_head_dirs"] = f32c(inputs["L0_attn_head_dirs"])
        h["L0_attn_scale"] = f32c(inputs["L0_attn_scale"])
        h["l0_W1T"] = np.ascontiguousarray(f32c(inputs["L0_ffn_W1"]).T)
        h["L0_ffn_b1"] = f32c(inputs["L0_ffn_b1"])
        h["l0_W2T"] = np.ascontiguousarray(f32c(inputs["L0_ffn_W2"]).T)
        h["L0_ffn_b2"] = f32c(inputs["L0_ffn_b2"])
    if 1 in layers:
        h["l1_WinT"] = np.ascontiguousarray(f32c(inputs["L1_attn_W_in"]).T)
        h["l1_WoutT"] = np.ascontiguousarray(f32c(inputs["L1_attn_W_out"]).T)
        for nm in ("L1_attn_b_in", "L1_attn_b_out", "L1_attn_Wc",
                   "L1_attn_A", "L1_attn_alphas", "L1_ffn_Wg",
                   "L1_ffn_b1", "L1_ffn_b2"):
            h[nm] = f32c(inputs[nm])
        h["l1_V_T"] = np.ascontiguousarray(
            f32c(inputs["L1_attn_V"]).transpose(2, 0, 1).reshape(128, 32))
        h["l1_base_T"] = np.ascontiguousarray(
            f32c(inputs["L1_attn_base"]).transpose(2, 0, 1).reshape(128, 168))
        h["l1_W1f"] = np.ascontiguousarray(
            f32c(inputs["L1_ffn_W1"]).reshape(8 * DM, 4096))
        h["l1_W2f"] = np.ascontiguousarray(
            f32c(inputs["L1_ffn_W2"]).reshape(8 * 4096, DM))
        s32s = []
        for core in range(NCORE):
            m = np.zeros((32, 5), dtype=np.float32)
            for r in range(NCORE):
                e0, e1, l0_, l1_ = r * 4 + 0, r * 4 + 1, r * 4 + 2, r * 4 + 3
                if r < core:
                    m[e0, 0] = 1.0
                    m[e1, 2] = 1.0
                    m[l0_, 3] = 1.0
                    m[l1_, 4] = 1.0
                if r != core:
                    m[e0, 3] = 1.0
                    m[e1, 4] = 1.0
                m[e0, 1] = 1.0
                m[l0_, 1] = 1.0
            s32s.append(m)
        p.host_percore["l1_S32"] = s32s
    if 2 in layers:
        h["l2_WqT"] = np.ascontiguousarray(f32c(inputs["L2_attn_Wq"]).T)
        h["l2_WkT"] = np.ascontiguousarray(f32c(inputs["L2_attn_Wk"]).T)
        h["l2_WvT"] = np.ascontiguousarray(f32c(inputs["L2_attn_Wv"]).T)
        h["l2_WoT"] = np.ascontiguousarray(f32c(inputs["L2_attn_Wo"]).T)
        for nm in ("L2_attn_bq", "L2_attn_bk", "L2_attn_bv", "L2_attn_bo",
                   "L2_ffn_w1_s", "L2_ffn_w2_s", "L2_ffn_h1_s",
                   "L2_ffn_h2_s", "L2_ffn_w3_s"):
            h[nm] = f32c(inputs[nm])
        h["l2_w1T"] = np.ascontiguousarray(f32c(inputs["L2_ffn_w1_w"]).T)
        h["l2_w2T"] = np.ascontiguousarray(f32c(inputs["L2_ffn_w2_w"]).T)
        h["l2_h1T"] = np.ascontiguousarray(f32c(inputs["L2_ffn_h1_w"]).T)
        h["l2_h2T"] = np.ascontiguousarray(f32c(inputs["L2_ffn_h2_w"]).T)
        h["l2_w3T"] = np.ascontiguousarray(f32c(inputs["L2_ffn_w3_w"]).T)
    if 3 in layers:
        for nm in ("L3_attn_Wq", "L3_attn_Wk", "L3_attn_Wv", "L3_attn_Wg",
                   "L3_attn_Wo", "L3_attn_sk_w1", "L3_attn_sv_w1",
                   "L3_attn_sk_w2", "L3_attn_sv_w2", "L3_attn_sk_b2",
                   "L3_attn_sv_b2", "L3_attn_gnorm_w"):
            h[nm] = f32c(inputs[nm])
        h["l3_upT"] = np.ascontiguousarray(f32c(inputs["L3_ffn_w_up"]).T)
        h["l3_downT"] = np.ascontiguousarray(f32c(inputs["L3_ffn_w_down"]).T)



def emit_dump(p, name, src_name, shape, dtype):
    nc = p.nc
    d_o = nc.dram_tensor(name, shape, dtype, kind="ExternalOutput")
    with nc.Block() as blk:
        @blk.gpsimd
        def _(g):
            p.dma(g, d_o.ap(), p.dram[src_name].ap(), "dP")
            p.wait_all(g, "dP")


def build(layers=(0, 1, 2, 3), dbg=None, inputs=None):
    nc = bass.Bass("TRN2", target_bir_lowering=False)
    p = P(nc)
    p.layers = layers
    declare_io(p)
    declare_weights(p, layers)
    p.host.update(_host_consts())
    if inputs is not None:
        prep_weights_host(p, inputs, layers)
    emit_setup(p)
    emit_entry(p)
    for li in layers:
        p.pfx = f"_{li}"
        emit_ln(p, li)
        EMIT_ATTN[li](p)
        if dbg == "branches":
            emit_dump(p, f"dbg_res_attn{li}", "x_res", [DM, T], F32)
            if "o_scr" in p.dram:
                emit_dump(p, f"dbg_oscr{li}", "o_scr", [DM, T], BF16)
        EMIT_FFN[li](p)
        if dbg == "moe" and li == 1:
            emit_dump(p, "dbg_xnt", "xn_t", [T, DM], BF16)
            emit_dump(p, "dbg_din", "l1_Din", [S_TOT + 128, 1],
                      mybir.dt.int32)
            emit_dump(p, "dbg_dout", "l1_Dout", [S_TOT + 128, 1],
                      mybir.dt.int32)
            emit_dump(p, "dbg_ymoe", "y_moe", [2 * T + 128, DM], F32)
        if dbg == "branches" and li == 0:
            emit_dump(p, "dbg_wq1", "l0_wq1_hbm", [DM, 4096], BF16)
        if dbg == "cmb" and li == 1:
            d_o = nc.dram_tensor("dbg_cmb", [8, T], BF16,
                                 kind="ExternalOutput")
            with nc.Block() as blk:
                @blk.gpsimd
                def _(g):
                    p.dma(g, d_o.ap(), p.sb["l1_cmb"][0:8, :], "dP")
                    p.wait_all(g, "dP")
    p.pfx = ""
    if dbg == "xn":
        d_o = nc.dram_tensor("dbg_xn", [DM, T], BF16, kind="ExternalOutput")
        with nc.Block() as blk:
            @blk.gpsimd
            def _(g):
                p.dma(g, d_o.ap(), p.dram["xn_hbm"].ap(), "dP")
                p.wait_all(g, "dP")
    emit_exit(p)
    for k, v in p.sems.v.items():
        assert v < 60000, f"semaphore channel {k} reaches {v} (16-bit risk)"
    nc.finalize()
    return p



def _prep_host(p, inputs):
    prep_weights_host(p, inputs, getattr(p, "layers", ()))
    in_maps = []
    x = np.ascontiguousarray(inputs["L0_attn_x"], dtype=np.float32)
    for c in range(NCORE):
        m = {k: np.ascontiguousarray(v) for k, v in p.host.items()}
        for k, lst in p.host_percore.items():
            m[k] = np.ascontiguousarray(lst[c])
        m["x_in"] = np.ascontiguousarray(x[[c, c + NCORE]])
        in_maps.append(m)
    return in_maps


def kernel(**inputs):
    p = build(layers=(0, 1, 2, 3))
    in_maps = _prep_host(p, inputs)
    res = run_bass_kernel_spmd(p.nc, in_maps, core_ids=list(range(NCORE)))
    out = np.empty((NCORE * NSEQ, SEQ, DM), dtype=np.float32)
    for c, r in enumerate(res.results):
        o = np.asarray(r["out"], dtype=np.float32)
        out[c] = o[0]
        out[c + NCORE] = o[1]
    return out
